# version 03
# baseline (speedup 1.0000x reference)
import math
import contextlib
import numpy as np
import ml_dtypes

import concourse.bass as bass
import concourse.mybir as mybir
from concourse.bass_utils import run_bass_kernel_spmd

F32 = mybir.dt.float32
BF16 = mybir.dt.bfloat16
AF = mybir.ActivationFunctionType
ALU = mybir.AluOpType
AX = mybir.AxisListType

E = 2048
FC = 16
TSEQ = 4096
TOWN = 1024
TEXT = 512
TSINK = 4
TP = TOWN + TEXT + TSINK
H = 16
MCN = 11
NEXP = 16
TOPK = 6
LAMBDA_INIT = 0.8 - 0.6 * math.exp(-0.3 * 12)
SCALE_A = 64.0 ** -0.5
SCALE_B = 128.0 ** -0.5
EPS = 1e-5
WINDOW = 512
SINKN = 4


def fchunks(total, step=512):
    out, o = [], 0
    while o < total:
        n = min(step, total - o)
        out.append((o, n))
        o += n
    return out


TCS = fchunks(TP)
TCS_OWN = fchunks(TOWN)
T128 = fchunks(TP, 128)
T128_OWN = fchunks(TOWN, 128)


NLANES = 8


class _Op:
    __slots__ = ("idx", "eng", "fn", "deps", "sem", "val", "dma", "dmaidx",
                 "dbg_r", "dbg_w")


class Prog:
    ENG = ("pe", "act", "dve", "pool", "sync")
    QUEUES = ("sync", "act", "pool")

    def __init__(self):
        self.ops = []
        self.res = {}
        self.fence_deps = set()
        self.dcnt_add = {q: 0 for q in self.QUEUES}

    def add(self, eng, fn, reads=(), writes=(), dma=False):
        deps = set(self.fence_deps)
        for r in reads:
            e = self.res.get(r)
            if e is not None and e[0] is not None:
                deps.add(e[0])
        for w in writes:
            e = self.res.get(w)
            if e is not None:
                if e[0] is not None:
                    deps.add(e[0])
                deps |= e[1]
        i = len(self.ops)
        op = _Op()
        op.idx, op.eng, op.fn, op.dma, op.sem = i, eng, fn, dma, None
        if dma:
            op.dmaidx = self.dcnt_add[eng]
            self.dcnt_add[eng] += 1
        op.deps = {d for d in deps if d != i}
        op.dbg_r, op.dbg_w = list(reads), list(writes)
        self.ops.append(op)
        for w in writes:
            self.res[w] = [i, set()]
        for r in reads:
            e = self.res.setdefault(r, [None, set()])
            e[1].add(i)
        return i

    def fence(self):
        last = {}
        for op in self.ops:
            key = (op.eng, op.dmaidx % NLANES) if op.dma else (op.eng, None)
            last[key] = op.idx
        self.fence_deps = set(last.values())

    def emit(self, block, engsem, dmasem):
        needed = set()
        for op in self.ops:
            needed |= op.deps
        cnt = {e: 0 for e in self.ENG}
        for op in self.ops:
            if op.dma:
                op.sem = dmasem[op.eng][op.dmaidx % NLANES]
                op.val = 16 * (op.dmaidx // NLANES + 1)
            elif op.idx in needed:
                cnt[op.eng] += 1
                op.sem, op.val = engsem[op.eng], cnt[op.eng]
        streams = {e: [] for e in self.ENG}
        for op in self.ops:
            streams[op.eng].append(op)
        allops = self.ops

        def runner(name):
            def run(eng):
                sat = {}
                for op in streams[name]:
                    waits = {}
                    for d in op.deps:
                        dop = allops[d]
                        k = id(dop.sem)
                        if sat.get(k, 0) >= dop.val:
                            continue
                        prev = waits.get(k)
                        waits[k] = (dop.sem,
                                    dop.val if prev is None else max(prev[1], dop.val))
                    if op.dma and op.val > 16:
                        k = id(op.sem)
                        pv = op.val - 16
                        if sat.get(k, 0) < pv and not (k in waits and waits[k][1] >= pv):
                            prev = waits.get(k)
                            waits[k] = (op.sem,
                                        pv if prev is None else max(prev[1], pv))
                    for k, (s, v) in waits.items():
                        eng.wait_ge(s, v)
                        sat[k] = v
                    try:
                        inst = op.fn(eng)
                    except Exception as ex:
                        raise RuntimeError(
                            "op %d on %s failed (reads=%s writes=%s): %s"
                            % (op.idx, name, getattr(op, "dbg_r", "?"),
                               getattr(op, "dbg_w", "?"), ex)) from ex
                    if op.sem is not None:
                        inst.then_inc(op.sem, 16 if op.dma else 1)
            return run

        block.tensor(runner("pe"))
        block.scalar(runner("act"))
        block.vector(runner("dve"))
        block.gpsimd(runner("pool"))
        block.sync(runner("sync"))



def _rhsT3(W):
    return np.ascontiguousarray(W.T.reshape(FC, 128, W.shape[0]))

def _rhsT3_em(W):
    return np.ascontiguousarray(W.reshape(FC, 128, W.shape[1]))

def _oslab(W):
    return np.ascontiguousarray(W.reshape(16, 128, 16, 128).transpose(0, 3, 2, 1))

def _oslab_em(W):
    return np.ascontiguousarray(W.reshape(16, 128, 16, 128).transpose(2, 1, 0, 3))

def _guslab(W):
    return np.ascontiguousarray(W.reshape(NEXP, MCN, 128, FC, 128).transpose(0, 1, 4, 3, 2))

def _dslab(W):
    return np.ascontiguousarray(W.reshape(NEXP, 16, 128, MCN, 128).transpose(0, 1, 4, 3, 2))


def _rope_a_tables(pos):
    i = np.arange(32, dtype=np.float64)
    ang = 1.0 / 10000.0 ** (i / 31.0)
    th = pos.astype(np.float64)[:, None] * ang[None, :]
    return (np.tile(np.cos(th), (1, 8)).astype(np.float32),
            np.tile(np.sin(th), (1, 8)).astype(np.float32))


def _rope_b_tables(pos):
    j = np.arange(64, dtype=np.float64)
    inv = 1.0 / 10000.0 ** (j / 64.0)
    th = pos.astype(np.float64)[:, None] * inv[None, :]
    return (np.tile(np.cos(th), (1, 4)).astype(np.float32),
            np.tile(np.sin(th), (1, 4)).astype(np.float32))


def _attnb_masks(j):
    base = j * TOWN
    lg = np.full(TP, -10 ** 9, dtype=np.int64)
    lg[0:TOWN] = base + np.arange(TOWN)
    if j > 0:
        lg[TOWN:TOWN + TEXT] = base - TEXT + np.arange(TEXT)
    lg[TOWN + TEXT:] = np.arange(TSINK)
    lo = base - TEXT if j > 0 else base
    canon = np.zeros(TP, dtype=bool)
    canon[0:TOWN + TEXT] = lg[0:TOWN + TEXT] >= 0
    for m in range(TSINK):
        if not (lo <= m < base + TOWN):
            canon[TOWN + TEXT + m] = True
    gq = base + np.arange(TOWN)
    masks = np.zeros((2, 13, 128, 512), dtype=np.float32)
    for qb in range(2):
        q = gq[qb * 512:(qb + 1) * 512]
        for kc in range(13):
            p0 = kc * 128
            npart = min(128, TP - p0)
            k = lg[p0:p0 + npart]
            ok = canon[p0:p0 + npart]
            m = (k[:, None] <= q[None, :]) & (
                (k[:, None] >= q[None, :] - WINDOW) | (k[:, None] < SINKN)) & ok[:, None]
            masks[qb, kc, :npart, :] = m.astype(np.float32)
    return masks



OVB = 57344


def build(taps=()):
    nc = bass.Bass("TRN2", target_bir_lowering=False)
    es = contextlib.ExitStack()

    def din(name, shape, dt=F32):
        return nc.dram_tensor(name, list(shape), dt, kind="ExternalInput").ap()

    x_fm = din("x_fm", (E, TSEQ))
    x_proc = din("x_proc", (E, TP))
    wqa = din("wqa", (FC, 128, E))
    wka = din("wka", (FC, 128, E))
    wva = din("wva", (FC, 128, E))
    woa = din("woa", (16, 128, 16, 128))
    wqb = din("wqb", (FC, 128, E))
    wkb = din("wkb", (FC, 128, 512))
    wvb = din("wvb", (FC, 128, 512))
    wob = din("wob", (16, 128, 16, 128))
    bqb = din("bqb", (1, E))
    bkb = din("bkb", (1, 512))
    bvb = din("bvb", (1, 512))
    bob = din("bob", (128, 16))
    gate_sl = din("gate_sl", (FC, 128, NEXP))
    wg = din("wg", (NEXP, MCN, 128, FC, 128))
    wu = din("wu", (NEXP, MCN, 128, FC, 128))
    wd = din("wd", (NEXP, 16, 128, MCN, 128))
    lq1 = din("lq1", (1, 64)); lk1 = din("lk1", (1, 64))
    lq2 = din("lq2", (1, 64)); lk2 = din("lk2", (1, 64))
    gam = din("gam", (1, 128))
    ident = din("ident", (128, 128))
    identb = din("identb", (128, 128), BF16)
    ca_seq = din("ca_seq", (TSEQ, 256), BF16); sa_seq = din("sa_seq", (TSEQ, 256), BF16)
    ca_pr = din("ca_pr", (TP, 256), BF16); sa_pr = din("sa_pr", (TP, 256), BF16)
    cb_pr = din("cb_pr", (TP, 256), BF16); sb_pr = din("sb_pr", (TP, 256), BF16)
    maskb = din("maskb", (2, 13, 128, 512), BF16)

    def dbf(name, shape):
        return nc.dram_tensor(name, list(shape), BF16, kind="Internal").ap()
    wka_b = dbf("wka_b", (FC, 128, E))
    wva_b = dbf("wva_b", (FC, 128, E))
    wqa_b = dbf("wqa_b", (FC, 128, E))
    woa_b = dbf("woa_b", (16, 128, 16, 128))
    wqb_b = dbf("wqb_b", (FC, 128, E))
    wkb_b = dbf("wkb_b", (FC, 128, 512))
    wvb_b = dbf("wvb_b", (FC, 128, 512))
    wob_b = dbf("wob_b", (16, 128, 16, 128))
    wg_b = dbf("wg_b", (NEXP, MCN, 128, FC, 128))
    wu_b = dbf("wu_b", (NEXP, MCN, 128, FC, 128))
    wd_b = dbf("wd_b", (NEXP, 16, 128, MCN, 128))

    y = nc.dram_tensor("y", [E, TOWN], F32, kind="ExternalOutput").ap()
    kvk = nc.dram_tensor("kvk", [E, TSEQ], BF16,
                         kind="ExternalOutput" if "kvk" in taps else "Internal").ap()
    kvv = nc.dram_tensor("kvv", [H, 32, 128, 128], BF16,
                         kind="ExternalOutput" if "kvv" in taps else "Internal").ap()
    afm = nc.dram_tensor("afm", [E, TP], BF16,
                         kind="ExternalOutput" if "afm" in taps else "Internal").ap()
    tap_outs = {}
    for t in taps:
        if t.startswith("x:"):
            tap_outs[t] = nc.dram_tensor("tap_" + t[2:], [E, TP], F32,
                                         kind="ExternalOutput").ap()

    sbt = lambda name, shape, dt: es.enter_context(nc.sbuf_tensor(name, list(shape), dt))
    xres = sbt("xres", (128, FC * TP), F32)
    xhat = sbt("xhat", (128, FC * TP), BF16)
    ov = sbt("ov", (128, OVB), mybir.dt.uint8)
    sm = sbt("sm", (128, 512), F32)
    smb = sbt("smb", (128, 272), BF16)
    gts = sbt("gts", (128, FC * NEXP), F32)
    ps = es.enter_context(nc.psum_tensor("ps", [128, 4096], F32))

    engsem = {e: es.enter_context(nc.semaphore("sem_" + e)) for e in Prog.ENG}
    dmasem = {q: [es.enter_context(nc.semaphore("dsem_%s_%d" % (q, l)))
                  for l in range(NLANES)] for q in Prog.QUEUES}
    block = es.enter_context(nc.Block())

    def ovf(byte, n):
        assert byte % 4 == 0 and byte + 4 * n <= OVB, (byte, n)
        return ov[:, byte: byte + 4 * n].bitcast(F32)

    def ovb(byte, n):
        assert byte % 2 == 0 and byte + 2 * n <= OVB, (byte, n)
        return ov[:, byte: byte + 2 * n].bitcast(BF16)

    def psv(bank, words):
        assert bank * 512 + words <= 4096
        return ps[:, bank * 512: bank * 512 + words]

    def psbf(bank, n):
        return ps[:, bank * 512: bank * 512 + (n + 1) // 2].bitcast(BF16)[:, 0:n]

    def pnames(bank, words):
        return ["ps%d" % b for b in range(bank, bank + (words + 511) // 512)]

    ONESR = sm[0:1, 0:128]
    ONESC = sm[:, 128:129]
    GAMN = sm[:, 129:257]
    LAMC = sm[:, 257:258]
    BA = lambda h: sm[:, 258 + h:259 + h]
    BB = lambda h: sm[:, 274 + h:275 + h]
    NKA = sm[:, 290:291]
    NKB = sm[:, 291:292]
    NQA = sm[:, 292:293]
    NQT = sm[:, 293:294]
    T0 = sm[0:1, 294:295]
    T1 = sm[0:1, 295:296]
    MAXK2A = sm[0:1, 296:297]
    MAXK2B = sm[0:1, 297:298]
    COLG = sm[0:16, 298:299]
    EP1 = sm[:, 299:300]
    NEGBIG = sm[:, 302:303]
    EPSC = sm[:, 303:304]
    LAMI = sm[:, 432:433]
    LAMI = sm[:, 498:499]
    EP2 = sm[:, 300:301]
    EP3 = sm[:, 301:302]
    IDENT_F = sm[:, 304:432]
    ROW64 = sm[0:1, 432:496]
    ROW64B = sm[0:1, 496:512]
    ONESC_BF = smb[:, 0:1]
    ONESR_BF = smb[0:1, 8:136]
    IDENT_BF = smb[:, 136:264]

    P = Prog()

    def dma(q, out_ap, in_ap, reads=(), writes=()):
        return P.add(q, lambda e, o=out_ap, i=in_ap: e.dma_start(out=o, in_=i),
                     reads=reads, writes=writes, dma=True)

    def mmg(pairs, reads=(), writes=(), start=True, stop=True, each_start=False):
        def fn(t, pairs=pairs, start=start, stop=stop, each_start=each_start):
            last, n = None, len(pairs)
            for k, (l, r, o) in enumerate(pairs):
                last = t.matmul(o, l, r,
                                start=(True if each_start else (start and k == 0)),
                                stop=(True if each_start else (stop and k == n - 1)))
            return last
        return P.add("pe", fn, reads=reads, writes=writes)

    act = lambda fn, reads=(), writes=(): P.add("act", fn, reads=reads, writes=writes)
    dve = lambda fn, reads=(), writes=(): P.add("dve", fn, reads=reads, writes=writes)
    pool = lambda fn, reads=(), writes=(): P.add("pool", fn, reads=reads, writes=writes)
    pe = lambda fn, reads=(), writes=(): P.add("pe", fn, reads=reads, writes=writes)

    XC = lambda fc, off=0, n=TP: xres[:, fc * TP + off: fc * TP + off + n]
    XH = lambda fc, off=0, n=TP: xhat[:, fc * TP + off: fc * TP + off + n]
    XN = lambda fc: "x%d" % fc
    XHN = lambda fc: "xh%d" % fc

    x_fm3 = x_fm.rearrange("(fc p) t -> p fc t", p=128)
    x_proc3 = x_proc.rearrange("(fc p) t -> p fc t", p=128)
    kvk3 = kvk.rearrange("(fc p) t -> p fc t", p=128)
    afm3 = afm.rearrange("(fc p) t -> p fc t", p=128)
    y3 = y.rearrange("(fc p) t -> p fc t", p=128)

    def ln_fm(tag, xin, xout, rnames, wnames, tcsl, rows, scr, inplace):
        MR_ = ovf(scr, 512)[0:1, :]
        VR_ = ovf(scr + 2048, 512)[0:1, :]
        M2_ = ovf(scr + 4096, 512)[0:1, :]
        XB_ = ovb(scr + 6144, 512)
        XQ_ = ovb(scr + 7168, 512)
        T1_ = None if inplace else ovf(scr + 8192, 512)
        rres, mrres = tag + "R", tag + "MR"
        for (off, n) in tcsl:
            if rows is not None:
                rrow = rows[0][0:1, off:off + n]
                mrrow = rows[1][0:1, off:off + n]
                rxtra, mxtra = [], []
            else:
                rrow = ovf(scr + 4096, n)[0:1, :]
                mrrow = ovf(scr + 6144, n)[0:1, :]
                rxtra, mxtra = [tag + "m2"], [tag + "xb", tag + "xq"]
            s_ps, q_ps = psv(4, n)[0:1, :], psv(5, n)[0:1, :]
            for fc in range(FC):
                dve(lambda e, fc=fc, off=off, n=n: e.tensor_copy(XB_[:, 0:n], xin(fc, off, n)),
                    reads=[rnames(fc)], writes=[tag + "xb"])
                act(lambda e, fc=fc, off=off, n=n: e.activation(XQ_[:, 0:n], xin(fc, off, n),
                                                                AF.Square),
                    reads=[rnames(fc)], writes=[tag + "xq"])
                mmg([(ONESC_BF, XB_[:, 0:n], s_ps)], reads=["onescb", tag + "xb"],
                    writes=pnames(4, n), start=(fc == 0), stop=(fc == FC - 1))
                mmg([(ONESC_BF, XQ_[:, 0:n], q_ps)], reads=["onescb", tag + "xq"],
                    writes=pnames(5, n), start=(fc == 0), stop=(fc == FC - 1))
            act(lambda e, n=n, s_ps=s_ps: e.activation(MR_[:, 0:n], s_ps, AF.Copy,
                                                       scale=1.0 / E),
                reads=pnames(4, n), writes=[tag + "m"])
            act(lambda e, n=n, q_ps=q_ps: e.activation(VR_[:, 0:n], q_ps, AF.Copy,
                                                       scale=1.0 / E),
                reads=pnames(5, n), writes=[tag + "v"])
            act(lambda e, n=n: e.activation(M2_[:, 0:n], MR_[:, 0:n], AF.Square),
                reads=[tag + "m"], writes=[tag + "m2"])
            dve(lambda e, n=n: e.tensor_tensor(VR_[:, 0:n], VR_[:, 0:n], M2_[:, 0:n],
                                               op=ALU.subtract),
                reads=[tag + "v", tag + "m2"], writes=[tag + "v"])
            act(lambda e, n=n: e.activation(VR_[:, 0:n], VR_[:, 0:n], AF.Sqrt,
                                            bias=EPSC[0:1, :]),
                reads=[tag + "v", "epsc"], writes=[tag + "v"])
            dve(lambda e, n=n, rrow=rrow: e.reciprocal(rrow, VR_[:, 0:n]),
                reads=[tag + "v"], writes=[rres] + rxtra)
            dve(lambda e, n=n, rrow=rrow, mrrow=mrrow: e.tensor_tensor(
                mrrow, MR_[:, 0:n], rrow, op=ALU.mult),
                reads=[tag + "m", rres], writes=[mrres] + mxtra)
            r_ps, m_ps = psv(6, n), psv(7, n)
            mmg([(ONESR, rrow, r_ps)], reads=["onesr", rres] + rxtra,
                writes=pnames(6, n))
            mmg([(ONESR, mrrow, m_ps)], reads=["onesr", mrres] + mxtra,
                writes=pnames(7, n))
            for fc in range(FC):
                if inplace:
                    dve(lambda e, fc=fc, off=off, n=n, r_ps=r_ps: e.tensor_tensor(
                        xin(fc, off, n), xin(fc, off, n), r_ps, op=ALU.mult),
                        reads=[rnames(fc)] + pnames(6, n), writes=[rnames(fc)])
                    dve(lambda e, fc=fc, off=off, n=n, m_ps=m_ps: e.tensor_tensor(
                        xout(fc, off, n), xin(fc, off, n), m_ps, op=ALU.subtract),
                        reads=[rnames(fc)] + pnames(7, n), writes=[wnames(fc)])
                else:
                    dve(lambda e, fc=fc, off=off, n=n, r_ps=r_ps: e.tensor_tensor(
                        T1_[:, 0:n], xin(fc, off, n), r_ps, op=ALU.mult),
                        reads=[rnames(fc)] + pnames(6, n), writes=[tag + "t1"])
                    dve(lambda e, fc=fc, off=off, n=n, m_ps=m_ps: e.tensor_tensor(
                        xout(fc, off, n), T1_[:, 0:n], m_ps, op=ALU.subtract),
                        reads=[tag + "t1"] + pnames(7, n), writes=[wnames(fc)])

    LN_SCR = 47104

    def ln_main(tag, tcsl, keep_rows):
        rows = (ovf(0, TP)[0:1, :], ovf(6160, TP)[0:1, :]) if keep_rows else None
        ln_fm(tag, lambda fc, off, n: XC(fc, off, n), lambda fc, off, n: XH(fc, off, n),
              XN, XHN, tcsl, rows, LN_SCR, False)

    def rope6(tag, xe, xo, c3, s3, oe, oo, rd, wr, t1, t2):
        dve(lambda e: e.tensor_tensor(t1, xe, c3, op=ALU.mult), reads=rd, writes=[tag + "1"])
        dve(lambda e: e.tensor_tensor(t2, xo, s3, op=ALU.mult), reads=rd, writes=[tag + "2"])
        dve(lambda e: e.tensor_tensor(oe, t1, t2, op=ALU.subtract),
            reads=[tag + "1", tag + "2"], writes=wr)
        dve(lambda e: e.tensor_tensor(t1, xe, s3, op=ALU.mult), reads=rd, writes=[tag + "1"])
        dve(lambda e: e.tensor_tensor(t2, xo, c3, op=ALU.mult), reads=rd, writes=[tag + "2"])
        dve(lambda e: e.tensor_tensor(oo, t1, t2, op=ALU.add),
            reads=[tag + "1", tag + "2"], writes=wr)

    def norm_acc(tag, src, nscr, col, first, rd, n=128):
        act(lambda e: e.activation(nscr, src, AF.Square, accum_out=NQT[0:n, :]),
            reads=rd, writes=[tag + "nscr", "nqt"])
        if first:
            dve(lambda e: e.tensor_copy(col[0:n, :], NQT[0:n, :]),
                reads=["nqt"], writes=[tag + "ncol"])
        else:
            dve(lambda e: e.tensor_tensor(col[0:n, :], col[0:n, :], NQT[0:n, :], op=ALU.max),
                reads=["nqt", tag + "ncol"], writes=[tag + "ncol"])

    def finalize_bound(tag, col, out_scalar, rowscr, outname):
        pe(lambda e: e.transpose(psv(3, 128)[0:1, :], col, IDENT_F),
           reads=[tag + "ncol", "identf"], writes=pnames(3, 128))
        act(lambda e: e.copy(rowscr[0:1, 0:128], psv(3, 128)[0:1, :]),
            reads=pnames(3, 128), writes=[tag + "nrow"])
        dve(lambda e: e.reduce_max(out_scalar, rowscr[0:1, 0:128], axis=AX.X),
            reads=[tag + "nrow"], writes=[outname])

    pool(lambda e: e.memset(ONESR, 1.0), writes=["onesr"])
    pool(lambda e: e.memset(NEGBIG, -1e5), writes=["negbig"])
    pool(lambda e: e.memset(EPSC, EPS), writes=["epsc"])
    pool(lambda e: e.memset(LAMI, LAMBDA_INIT), writes=["lami"])
    pool(lambda e: e.memset(ONESC, 1.0), writes=["onesc"])
    pool(lambda e: e.memset(ONESC_BF, 1.0), writes=["onescb"])
    pool(lambda e: e.memset(ONESR_BF, 1.0), writes=["onesrb"])
    dma("sync", IDENT_F, ident, writes=["identf"])
    dma("sync", IDENT_BF, identb, writes=["identb"])
    dma("sync", gts[:].rearrange("p (ec o) -> p ec o", ec=FC),
        gate_sl.rearrange("ec p o -> p ec o"), writes=["gts"])

    dma("sync", ovf(0, 128)[0:1, :], gam, writes=["gamrow"])
    mmg([(ONESR, ovf(0, 128)[0:1, :], psv(0, 128))], reads=["onesr", "gamrow"],
        writes=pnames(0, 128))
    act(lambda e: e.activation(GAMN, psv(0, 128), AF.Copy, scale=-(1.0 - LAMBDA_INIT)),
        reads=pnames(0, 128), writes=["gamn"])

    dma("sync", ROW64, lq1, writes=["l1"])
    dma("sync", ovf(512, 64)[0:1, :], lk1, writes=["l2"])
    dma("sync", ovf(768, 64)[0:1, :], lq2, writes=["l3"])
    dma("sync", ovf(1024, 64)[0:1, :], lk2, writes=["l4"])
    dve(lambda e: e.tensor_tensor(ROW64, ROW64, ovf(512, 64)[0:1, :], op=ALU.mult),
        reads=["l1", "l2"], writes=["l1"])
    dve(lambda e: e.tensor_tensor(ovf(768, 64)[0:1, :], ovf(768, 64)[0:1, :],
                                  ovf(1024, 64)[0:1, :], op=ALU.mult),
        reads=["l3", "l4"], writes=["l3"])
    dve(lambda e: e.reduce_sum(T0, ROW64, axis=AX.X), reads=["l1"], writes=["t0"])
    dve(lambda e: e.reduce_sum(T1, ovf(768, 64)[0:1, :], axis=AX.X),
        reads=["l3"], writes=["t1"])
    act(lambda e: e.activation(T0, T0, AF.Exp), reads=["t0"], writes=["t0"])
    act(lambda e: e.activation(T1, T1, AF.Exp), reads=["t1"], writes=["t1"])
    dve(lambda e: e.tensor_tensor(T0, T0, T1, op=ALU.subtract),
        reads=["t0", "t1"], writes=["t0"])
    dve(lambda e: e.tensor_tensor(T0, T0, LAMI[0:1, :], op=ALU.add),
        reads=["t0", "lami"], writes=["t0"])
    mmg([(ONESR, T0, psv(0, 1))], reads=["onesr", "t0"], writes=pnames(0, 1))
    act(lambda e: e.copy(LAMC, psv(0, 1)), reads=pnames(0, 1), writes=["lam"])

    mmg([(gts[:, ec * NEXP:(ec + 1) * NEXP], ONESC, psv(1, 1)[0:16, :]) for ec in range(FC)],
        reads=["gts", "onesc"], writes=pnames(1, 1))
    act(lambda e: e.copy(COLG, psv(1, 1)[0:16, :]), reads=pnames(1, 1), writes=["colg"])

    P.fence()

    STG_IN = lambda r: xres[:, 18480 + r * 1024: 18480 + (r + 1) * 1024]
    STG_OUT = lambda r: xres[:, 22576 + r * 512: 22576 + (r + 1) * 512].bitcast(BF16)
    STG_NAMES = ["sgi%d" % r for r in range(4)] + ["sgo%d" % r for r in range(4)]
    _sri = [0]

    def stg(src_ap, dst_ap, ncols, name):
        for o in range(0, ncols, 1024):
            n = min(1024, ncols - o)
            r = _sri[0] % 4
            _sri[0] += 1
            dma("act", STG_IN(r)[:, 0:n], src_ap[:, o:o + n], writes=["sgi%d" % r])
            P.add("pool", lambda e, r=r, n=n: e.tensor_copy(STG_OUT(r)[:, 0:n],
                                                            STG_IN(r)[:, 0:n]),
                  reads=["sgi%d" % r], writes=["sgo%d" % r])
            dma("pool", dst_ap[:, o:o + n], STG_OUT(r)[:, 0:n],
                reads=["sgo%d" % r], writes=[name])

    for fc in range(FC):
        stg(wka[fc], wka_b[fc], E, "stg.wka")
        stg(wva[fc], wva_b[fc], E, "stg.wva")
    for fc in range(FC):
        stg(wqa[fc], wqa_b[fc], E, "stg.wqa")
    for oc in range(16):
        stg(woa[oc].rearrange("p a b -> p (a b)"),
            woa_b[oc].rearrange("p a b -> p (a b)"), E, "stg.woa")
    for ex in range(NEXP):
        for mc in range(MCN):
            stg(wg[ex, mc].rearrange("p a b -> p (a b)"),
                wg_b[ex, mc].rearrange("p a b -> p (a b)"), E, "stg.wg")
            stg(wu[ex, mc].rearrange("p a b -> p (a b)"),
                wu_b[ex, mc].rearrange("p a b -> p (a b)"), E, "stg.wu")
        for oc in range(16):
            stg(wd[ex, oc].rearrange("p a b -> p (a b)"),
                wd_b[ex, oc].rearrange("p a b -> p (a b)"), MCN * 128, "stg.wd")
    for fc in range(FC):
        stg(wqb[fc], wqb_b[fc], E, "stg.wqb")
        stg(wkb[fc], wkb_b[fc], 512, "stg.wkb")
        stg(wvb[fc], wvb_b[fc], 512, "stg.wvb")
    for oc in range(16):
        stg(wob[oc].rearrange("p a b -> p (a b)"),
            wob_b[oc].rearrange("p a b -> p (a b)"), E, "stg.wob")

    A_XT = lambda fc: ovf(fc * 2048, 512)
    A_XH = lambda fc: ovb(32768 + fc * 1024, 512)
    A_KT = lambda i: ovb(i * 4096, 2048)
    A_KRAW = ovb(16384, 512)
    A_RS1 = ovb(17408, 256)
    A_RS2 = ovb(17920, 256)
    A_VST = lambda r: ovb(18432 + r * 1024, 512)
    A_KFM = lambda r: ovb(20480 + r * 1024, 512)
    A_NSC = ovb(22528, 512)
    A_TC = ovb(23552, 256)
    A_TS = ovb(24064, 256)
    A_WS = lambda r: ovb(24576 + r * 4096, 2048)

    def a_ktnames(i):
        return ["A.xt%d" % (2 * i), "A.xt%d" % (2 * i + 1)]

    A_KWORK = (["A.ws0", "A.ws1", "A.kraw", "A.vst0", "A.vst1", "A.kfm0", "A.kfm1",
                "A.nsc", "A.tc", "A.ts", "A.r1", "A.r2"])
    for c in range(8):
        dma("sync", ov[:, 0:32768].bitcast(F32).rearrange("p (fc t) -> p fc t", fc=FC),
            x_fm3[:, :, c * 512:(c + 1) * 512],
            writes=["A.xt%d" % fc for fc in range(FC)] + A_KWORK)
        ln_fm("A.ln", lambda fc, off, n: A_XT(fc)[:, off:off + n],
              lambda fc, off, n: A_XH(fc)[:, off:off + n],
              lambda fc: "A.xt%d" % fc, lambda fc: "A.xh%d" % fc,
              [(0, 512)], None, 49152, True)
        for mat, wsrc, stgn, is_k in (("k", wka_b, "stg.wka", True),
                                      ("v", wva_b, "stg.wva", False)):
            for o5 in range(4):
                for qh in range(4):
                    r = qh % 2
                    dma("sync", A_WS(r).rearrange("p (ec o) -> p ec o", ec=4),
                        wsrc[qh * 4:(qh + 1) * 4, :, o5 * 512:(o5 + 1) * 512]
                        .rearrange("ec p o -> p ec o"), reads=[stgn],
                        writes=["A.ws%d" % r,
                                "A.xt%d" % (12 + 2 * r), "A.xt%d" % (13 + 2 * r)])
                    for i in range(4):
                        mmg([(A_XH(qh * 4 + k2)[:, i * 128:(i + 1) * 128],
                              A_WS(r)[:, k2 * 512:(k2 + 1) * 512], psv(i, 512))
                             for k2 in range(4)],
                            reads=["A.ws%d" % r] + ["A.xh%d" % (qh * 4 + k2)
                                                    for k2 in range(4)],
                            writes=pnames(i, 512),
                            start=(qh == 0), stop=(qh == 3))
                if is_k:
                    for i in range(4):
                        act(lambda e, i=i: e.copy(A_KRAW, psv(i, 512)),
                            reads=pnames(i, 512), writes=["A.kraw"])
                        norm_acc("A", A_KRAW, A_NSC, NKA,
                                 first=(c == 0 and o5 == 0 and i == 0), rd=["A.kraw"])
                        dma("sync", A_TC[0:128, :],
                            ca_seq[c * 512 + i * 128: c * 512 + (i + 1) * 128, :],
                            writes=["A.tc"])
                        dma("sync", A_TS[0:128, :],
                            sa_seq[c * 512 + i * 128: c * 512 + (i + 1) * 128, :],
                            writes=["A.ts"])
                        v4 = A_KRAW.rearrange("p (rh f two) -> p rh f two", rh=8, two=2)
                        ko = A_KT(i)[:, o5 * 512:(o5 + 1) * 512].rearrange(
                            "p (rh f two) -> p rh f two", rh=8, two=2)
                        c3 = A_TC.rearrange("p (rh f) -> p rh f", rh=8)
                        s3 = A_TS.rearrange("p (rh f) -> p rh f", rh=8)
                        t1 = A_RS1.rearrange("p (rh f) -> p rh f", rh=8)
                        t2 = A_RS2.rearrange("p (rh f) -> p rh f", rh=8)
                        rope6("A.r", v4[:, :, :, 0], v4[:, :, :, 1], c3, s3,
                              ko[:, :, :, 0], ko[:, :, :, 1],
                              rd=["A.kraw", "A.tc", "A.ts"], wr=a_ktnames(i),
                              t1=t1, t2=t2)
                else:
                    for i in range(4):
                        vr = i % 2
                        act(lambda e, i=i, vr=vr: e.copy(A_VST(vr), psv(i, 512)),
                            reads=pnames(i, 512), writes=["A.vst%d" % vr])
                        dma("act", kvv[o5 * 4:(o5 + 1) * 4, c * 4 + i, :, :]
                            .rearrange("h p d -> p h d"),
                            A_VST(vr).rearrange("p (h d) -> p h d", h=4),
                            reads=["A.vst%d" % vr], writes=["kvv"])
            if is_k:
                for fc in range(FC):
                    kr = fc % 2
                    for i in range(4):
                        pe(lambda e, fc=fc, i=i: e.transpose(
                            psbf(4, 512)[:, i * 128:(i + 1) * 128],
                            A_KT(i)[:, fc * 128:(fc + 1) * 128], IDENT_BF),
                           reads=a_ktnames(i) + ["identb"], writes=["ps4"])
                    act(lambda e, kr=kr: e.copy(A_KFM(kr), psbf(4, 512)),
                        reads=["ps4"], writes=["A.kfm%d" % kr])
                    dma("act", kvk3[:, fc, c * 512:(c + 1) * 512], A_KFM(kr),
                        reads=["A.kfm%d" % kr], writes=["kvk"])
    finalize_bound("A", NKA, MAXK2A, ovf(49152, 128), "maxk2a")
    P.fence()

    dma("sync", xres[:].rearrange("p (fc t) -> p fc t", fc=FC), x_proc3,
        writes=["x%d" % fc for fc in range(FC)] + STG_NAMES)
    ln_main("L1.", TCS, False)
    P.fence()

    C_QH = lambda r: ovb(r * 3080, 1540)
    C_PT = lambda r: ovb(6160 + r * 1032, 516)
    C_KBH = ovb(8224, 4096)
    C_VBH = ovb(16416, 4128)
    C_WQS = lambda r: ovb(24672 + r * 4096, 2048)
    C_QTM = ovb(32864, 1664)
    C_QRAW = lambda r: ovb(36192 + r * 256, 128)
    C_TC = lambda r: ovb(36704 + r * 512, 256)
    C_TS = lambda r: ovb(37728 + r * 512, 256)
    C_AOS = lambda r: ovb(38752 + r * 1024, 512)
    C_O1N = ovf(40800, 128)
    C_O2N = ovf(41312, 128)
    C_SQS = ovf(41824, 128)
    C_NSC = ovb(42336, 128)
    C_RS1 = ovb(42592, 64)
    C_RS2 = ovb(42720, 64)

    pool(lambda e: e.memset(C_VBH, 1.0), writes=["C.vbh"])

    BLKC = TCS

    def c_otile(s, qi):
        base = 4 + 2 * s
        woff = qi * 129 if qi < 3 else 512 + (qi - 3) * 129
        return psv(base, 1024)[:, woff:woff + 129]

    for h in range(H):
        hr = h % 2
        dma("act", C_WQS(hr).rearrange("p (ec o) -> p ec o", ec=FC),
            wqa_b[:, :, h * 128:(h + 1) * 128].rearrange("ec p o -> p ec o"),
            reads=["stg.wqa"], writes=["C.wqs%d" % hr])
        for i, (off, n) in enumerate(T128):
            pk = i % 2
            ir = i % 2
            mmg([(XH(ec, off, n), C_WQS(hr)[:, ec * 128:(ec + 1) * 128],
                  psv(pk, 128)[0:n, :]) for ec in range(FC)],
                reads=["C.wqs%d" % hr] + [XHN(ec) for ec in range(FC)],
                writes=pnames(pk, 128))
            act(lambda e, pk=pk, n=n, ir=ir: e.copy(C_QRAW(ir)[0:n, :], psv(pk, 128)[0:n, :]),
                reads=pnames(pk, 128), writes=["C.qraw%d" % ir])
            norm_acc("C", C_QRAW(ir)[0:n, :], C_NSC[0:n, :], NQA, first=(i == 0),
                     rd=["C.qraw%d" % ir], n=n)
            dma("act", C_TC(ir)[0:n, :], ca_pr[off:off + n, :], writes=["C.tc%d" % ir])
            dma("act", C_TS(ir)[0:n, :], sa_pr[off:off + n, :], writes=["C.ts%d" % ir])
            v4 = C_QRAW(ir)[0:n, :].rearrange("p (rh f two) -> p rh f two", rh=2, two=2)
            qo = C_QTM[0:n, i * 128:(i + 1) * 128].rearrange(
                "p (rh f two) -> p rh f two", rh=2, two=2)
            c3 = C_TC(ir)[0:n, 0:64].rearrange("p (rh f) -> p rh f", rh=2)
            s3 = C_TS(ir)[0:n, 0:64].rearrange("p (rh f) -> p rh f", rh=2)
            t1 = C_RS1[0:n, :].rearrange("p (rh f) -> p rh f", rh=2)
            t2 = C_RS2[0:n, :].rearrange("p (rh f) -> p rh f", rh=2)
            rope6("C.r", v4[:, :, :, 0], v4[:, :, :, 1], c3, s3,
                  qo[:, :, :, 0], qo[:, :, :, 1],
                  rd=["C.qraw%d" % ir, "C.tc%d" % ir, "C.ts%d" % ir],
                  wr=["C.qtm%d" % i], t1=t1, t2=t2)
        for i, (off, n) in enumerate(T128):
            pe(lambda e, i=i, n=n: e.transpose(psbf(2, 256)[:, 0:n],
                                               C_QTM[0:n, i * 128:(i + 1) * 128],
                                               IDENT_BF[0:n, 0:n]),
               reads=["C.qtm%d" % i, "identb"], writes=["ps2"])
            act(lambda e, i=i, off=off, n=n, hr=hr: e.copy(C_QH(hr)[:, off:off + n],
                                                           psbf(2, 256)[:, 0:n]),
                reads=["ps2"], writes=["C.qh%d" % hr])
        finalize_bound("C", NQA, T0, ovf(43072, 128), "C.max2")
        dve(lambda e: e.tensor_tensor(T0, T0, MAXK2A, op=ALU.mult),
            reads=["C.max2", "maxk2a"], writes=["C.b"])
        act(lambda e: e.activation(T0, T0, AF.Sqrt), reads=["C.b"], writes=["C.b"])
        act(lambda e: e.activation(T0, T0, AF.Copy, scale=-SCALE_A),
            reads=["C.b"], writes=["C.b"])
        mmg([(ONESR, T0, psv(3, 1))], reads=["onesr", "C.b"], writes=pnames(3, 1))
        act(lambda e, h=h: e.copy(BA(h), psv(3, 1)), reads=pnames(3, 1),
            writes=["Ba%d" % h])
        dma("sync", C_KBH, kvk[h * 128:(h + 1) * 128, :], reads=["kvk"], writes=["C.kbh"])
        dma("sync", C_VBH.rearrange("p (c d) -> p c d", c=32)[:, :, 0:128],
            kvv[h, :, :, :].rearrange("c p d -> p c d"),
            reads=["kvv", "C.vbh"], writes=["C.vbh"])
        for (qoff, qn) in BLKC:
            qcl = fchunks(qn, 128)
            for s in (0, 1):
                for tkc in range(32):
                    sr = tkc % 2
                    s_ps = psv(2 * sr, qn)
                    mmg([(C_KBH[s * 64:(s + 1) * 64, tkc * 128:(tkc + 1) * 128],
                          C_QH(hr)[s * 64:(s + 1) * 64, qoff + o2:qoff + o2 + n2],
                          s_ps[:, o2:o2 + n2]) for (o2, n2) in fchunks(qn)],
                        reads=["C.kbh", "C.qh%d" % hr], writes=pnames(2 * sr, qn),
                        each_start=True)
                    act(lambda e, sr=sr, qn=qn, s_ps=s_ps, h=h: e.activation(
                        C_PT(sr)[:, 0:qn], s_ps, AF.Exp, bias=BA(h), scale=SCALE_A),
                        reads=pnames(2 * sr, qn) + ["Ba%d" % h], writes=["C.pt%d" % sr])

                    def pvfn(t, sr=sr, s=s, tkc=tkc, qcl=qcl):
                        last = None
                        nq = len(qcl)
                        for qi, (o2, n2) in enumerate(qcl):
                            last = t.matmul(c_otile(s, qi)[0:n2, :],
                                            C_PT(sr)[:, o2:o2 + n2],
                                            C_VBH[:, tkc * 129:(tkc + 1) * 129],
                                            start=(tkc == 0 and qi in (0, 3)),
                                            stop=(tkc == 31 and
                                                  qi in (min(2, nq - 1), nq - 1)))
                        return last
                    P.add("pe", pvfn, reads=["C.pt%d" % sr, "C.vbh"],
                          writes=pnames(4 + 2 * s, 1024))
            for qi, (o2, n2) in enumerate(qcl):
                o1, o2t = c_otile(0, qi), c_otile(1, qi)
                dve(lambda e, o1=o1, n2=n2: e.reciprocal(EP1[0:n2, :], o1[0:n2, 128:129]),
                    reads=pnames(4, 1024), writes=["C.e1"])
                dve(lambda e, o2t=o2t, n2=n2: e.reciprocal(EP2[0:n2, :], o2t[0:n2, 128:129]),
                    reads=pnames(6, 1024), writes=["C.e2"])
                dve(lambda e, o1=o1, n2=n2: e.tensor_scalar(
                    C_O1N[0:n2, :], o1[0:n2, 0:128], EP1[0:n2, :], None, op0=ALU.mult),
                    reads=pnames(4, 1024) + ["C.e1"], writes=["C.o1n"])
                dve(lambda e, o2t=o2t, n2=n2: e.tensor_scalar(
                    C_O2N[0:n2, :], o2t[0:n2, 0:128], EP2[0:n2, :], None, op0=ALU.mult),
                    reads=pnames(6, 1024) + ["C.e2"], writes=["C.o2n"])
                dve(lambda e, n2=n2: e.scalar_tensor_tensor(
                    C_O2N[0:n2, :], C_O2N[0:n2, :], LAMC[0:n2, :], C_O1N[0:n2, :],
                    op0=ALU.mult, op1=ALU.subtract),
                    reads=["C.o1n", "C.o2n", "lam"], writes=["C.o2n"])
                act(lambda e, n2=n2: e.activation(C_SQS[0:n2, :], C_O2N[0:n2, :], AF.Square,
                                                  accum_out=EP3[0:n2, :]),
                    reads=["C.o2n"], writes=["C.ep3", "C.sqs"])
                act(lambda e, n2=n2: e.activation(EP3[0:n2, :], EP3[0:n2, :], AF.Sqrt,
                                                  bias=EPSC[0:n2, :], scale=1.0 / 128),
                    reads=["C.ep3", "epsc"], writes=["C.ep3"])
                dve(lambda e, n2=n2: e.reciprocal(EP3[0:n2, :], EP3[0:n2, :]),
                    reads=["C.ep3"], writes=["C.ep3"])
                dve(lambda e, n2=n2: e.tensor_scalar(
                    C_O2N[0:n2, :], C_O2N[0:n2, :], EP3[0:n2, :], None, op0=ALU.mult),
                    reads=["C.ep3", "C.o2n"], writes=["C.o2n"])
                ar = qi % 2
                dve(lambda e, n2=n2, ar=ar: e.tensor_tensor(
                    C_AOS(ar)[0:n2, 0:128], C_O2N[0:n2, :], GAMN[0:n2, :], op=ALU.mult),
                    reads=["C.o2n", "gamn"], writes=["C.ao%d" % ar])
                pe(lambda e, ar=ar, n2=n2: e.transpose(psbf(2, 256)[:, 0:n2],
                                                       C_AOS(ar)[0:n2, 0:128],
                                                       IDENT_BF[0:n2, 0:n2]),
                   reads=["C.ao%d" % ar, "identb"], writes=["ps2"])
                act(lambda e, ar=ar, n2=n2: e.copy(C_AOS(ar)[:, 256:256 + n2],
                                                   psbf(2, 256)[:, 0:n2]),
                    reads=["ps2"], writes=["C.aot%d" % ar])
                dma("act", afm[h * 128:(h + 1) * 128, qoff + o2:qoff + o2 + n2],
                    C_AOS(ar)[:, 256:256 + n2],
                    reads=["C.aot%d" % ar], writes=["afm"])
    P.fence()

    for fc in range(FC):
        dma("sync", XH(fc), afm3[:, fc, :], reads=["afm"], writes=[XHN(fc)])
    for oc in range(16):
        orr = oc % 2
        dma("act", C_WQS(orr).rearrange("p (fc o) -> p fc o", fc=FC), woa_b[oc],
            reads=["stg.woa"], writes=["D.ws%d" % orr])
        for bi, (off, n) in enumerate(TCS):
            pk = bi % 2
            mmg([(C_WQS(orr)[:, fc * 128:(fc + 1) * 128], XH(fc, off, n),
                  psv(2 * pk, n)) for fc in range(FC)],
                reads=["D.ws%d" % orr] + [XHN(fc) for fc in range(FC)],
                writes=pnames(2 * pk, n))
            dve(lambda e, oc=oc, off=off, n=n, pk=pk: e.tensor_tensor(
                XC(oc, off, n), XC(oc, off, n), psv(2 * pk, n), op=ALU.add),
                reads=pnames(2 * pk, n) + [XN(oc)], writes=[XN(oc)])
    if "x:attn_a" in taps:
        for fc in range(FC):
            dma("sync", tap_outs["x:attn_a"].rearrange("(fc p) t -> p fc t", p=128)[:, fc, :],
                XC(fc), reads=[XN(fc)], writes=["tapxa"])
    P.fence()

    FA = lambda mc: ovb(mc * 3080, 1540)
    F_R = ovf(0, TP)[0:1, :]
    F_MR = ovf(6160, TP)[0:1, :]
    F_LGT = ovf(12320, TP)[0:16, :]
    F_SC = ovf(18480, 16)
    F_SCW = ovf(18544, 16)
    F_MX = ovf(18608, 1)
    F_GE = ovf(18612, 16)
    F_CWTM = lambda i: ovf(18676 + i * 64, 16)
    F_GSL = lambda r: ovb(33880 + r * 1024, 512)
    F_USL = lambda r: ovb(35928 + r * 1024, 512)
    F_DSL = ovb(37976, 1408)
    F_CWTF = ovb(40792, 1540)
    F_CWB = ovb(43872, 1540)
    F_SG = ovb(LN_SCR + 6144, 512)
    F_ROWT = ovb(LN_SCR, 1540)

    def gate_and_moe(tag, ncols, tcsl, t128l):
        CWT = F_CWTF[0:16, 0:ncols]
        for (off, n) in tcsl:
            mmg([(gts[:, ec * NEXP:(ec + 1) * NEXP], XC(ec, off, n),
                  psv(4, n)[0:NEXP, :]) for ec in range(FC)],
                reads=["gts"] + [XN(ec) for ec in range(FC)], writes=pnames(4, n))
            dve(lambda e, off=off, n=n: e.tensor_copy(F_LGT[:, off:off + n],
                                                      psv(4, n)[0:NEXP, :]),
                reads=pnames(4, n), writes=[tag + "lg"])
        for (off, n) in tcsl:
            mmg([(ONESR[:, 0:16], F_R[0:1, off:off + n], psv(4, n)[0:NEXP, :])],
                reads=["onesr", tag + "R"], writes=pnames(4, n))
            dve(lambda e, off=off, n=n: e.tensor_tensor(
                F_LGT[:, off:off + n], F_LGT[:, off:off + n], psv(4, n)[0:NEXP, :],
                op=ALU.mult),
                reads=pnames(4, n) + [tag + "lg"], writes=[tag + "lg"])
            mmg([(ONESR[:, 0:16], F_MR[0:1, off:off + n], psv(4, n)[0:NEXP, :])],
                reads=["onesr", tag + "MR"], writes=pnames(4, n))
            dve(lambda e, off=off, n=n: e.tensor_scalar(
                psv(6, n)[0:NEXP, :], psv(4, n)[0:NEXP, :], COLG, None, op0=ALU.mult),
                reads=pnames(4, n) + ["colg"], writes=pnames(6, n))
            dve(lambda e, off=off, n=n: e.tensor_tensor(
                F_LGT[:, off:off + n], F_LGT[:, off:off + n], psv(6, n)[0:NEXP, :],
                op=ALU.subtract),
                reads=pnames(6, n) + [tag + "lg"], writes=[tag + "lg"])
        for i, (off, n) in enumerate(t128l):
            pe(lambda e, off=off, n=n: e.transpose(psv(4, 16)[0:n, :],
                                                   F_LGT[:, off:off + n],
                                                   IDENT_F[0:16, 0:16]),
               reads=[tag + "lg", "identf"], writes=pnames(4, 16))
            act(lambda e, n=n: e.activation(F_SC[0:n, :], psv(4, 16)[0:n, :], AF.Exp),
                reads=pnames(4, 16), writes=[tag + "e"])
            dve(lambda e, n=n: e.reduce_sum(F_MX[0:n, :], F_SC[0:n, :], axis=AX.X),
                reads=[tag + "e"], writes=[tag + "s"])
            dve(lambda e, n=n: e.reciprocal(F_MX[0:n, :], F_MX[0:n, :]),
                reads=[tag + "s"], writes=[tag + "s"])
            dve(lambda e, n=n: e.tensor_scalar(F_SC[0:n, :], F_SC[0:n, :], F_MX[0:n, :],
                                               None, op0=ALU.mult),
                reads=[tag + "s", tag + "e"], writes=[tag + "e"])
            dve(lambda e, n=n: e.tensor_copy(F_SCW[0:n, :], F_SC[0:n, :]),
                reads=[tag + "e"], writes=[tag + "w"])
            for it in range(TOPK):
                dve(lambda e, n=n: e.reduce_max(F_MX[0:n, :], F_SCW[0:n, :], axis=AX.X),
                    reads=[tag + "w"], writes=[tag + "mx"])
                if it < TOPK - 1:
                    dve(lambda e, n=n: e.tensor_scalar(F_GE[0:n, :], F_SCW[0:n, :],
                                                       F_MX[0:n, :], None, op0=ALU.is_ge),
                        reads=[tag + "w", tag + "mx"], writes=[tag + "g"])
                    dve(lambda e, n=n: e.scalar_tensor_tensor(
                        F_SCW[0:n, :], F_GE[0:n, :], NEGBIG[0:n, :], F_SCW[0:n, :],
                        op0=ALU.mult, op1=ALU.add),
                        reads=[tag + "g", tag + "w", "negbig"], writes=[tag + "w"])
            dve(lambda e, n=n: e.tensor_scalar(F_GE[0:n, :], F_SC[0:n, :], F_MX[0:n, :],
                                               None, op0=ALU.is_ge),
                reads=[tag + "e", tag + "mx"], writes=[tag + "g"])
            dve(lambda e, i=i, n=n: e.tensor_tensor(F_CWTM(i)[0:n, :], F_SC[0:n, :],
                                                    F_GE[0:n, :], op=ALU.mult),
                reads=[tag + "g", tag + "e"], writes=[tag + "cw%d" % i])
        for i, (off, n) in enumerate(t128l):
            pe(lambda e, i=i, n=n: e.transpose(psv(4, 128)[0:16, 0:n], F_CWTM(i)[0:n, :],
                                               IDENT_F[0:n, 0:n]),
               reads=[tag + "cw%d" % i, "identf"], writes=pnames(4, 128))
            act(lambda e, off=off, n=n: e.copy(CWT[:, off:off + n], psv(4, 128)[0:16, 0:n]),
                reads=pnames(4, 128), writes=[tag + "cwT"])
        P.fence()
        for ex in range(NEXP):
            mmg([(IDENT_BF[0:16, ex:ex + 1], CWT[:, o2:o2 + n2],
                  psv(4, ncols)[0:1, o2:o2 + n2]) for (o2, n2) in tcsl],
                reads=[tag + "cwT", "identb"], writes=pnames(4, ncols),
                each_start=True)
            act(lambda e: e.copy(F_ROWT[0:1, 0:ncols], psv(4, ncols)[0:1, :]),
                reads=pnames(4, ncols), writes=[tag + "rowT"])
            mmg([(ONESR_BF, F_ROWT[0:1, o2:o2 + n2], psv(4, ncols)[:, o2:o2 + n2])
                 for (o2, n2) in tcsl],
                reads=[tag + "rowT", "onesrb"], writes=pnames(4, ncols),
                each_start=True)
            dve(lambda e: e.tensor_copy(F_CWB[:, 0:ncols], psv(4, ncols)),
                reads=pnames(4, ncols), writes=[tag + "cwb"])
            for mc in range(MCN):
                for qh in range(4):
                    r = qh % 2
                    dma("sync", F_GSL(r).rearrange("p (ec m) -> p ec m", ec=4),
                        wg_b[ex, mc, :, qh * 4:(qh + 1) * 4, :].rearrange("p ec m -> p ec m"),
                        reads=["stg.wg"], writes=[tag + "gs%d" % r])
                    dma("act", F_USL(r).rearrange("p (ec m) -> p ec m", ec=4),
                        wu_b[ex, mc, :, qh * 4:(qh + 1) * 4, :].rearrange("p ec m -> p ec m"),
                        reads=["stg.wu"], writes=[tag + "us%d" % r])
                    for ti, (off, n) in enumerate(tcsl):
                        mmg([(F_GSL(r)[:, k2 * 128:(k2 + 1) * 128],
                              XH(qh * 4 + k2, off, n), psv(ti, n)) for k2 in range(4)],
                            reads=[tag + "gs%d" % r] + [XHN(qh * 4 + k2) for k2 in range(4)],
                            writes=pnames(ti, n), start=(qh == 0), stop=(qh == 3))
                        mmg([(F_USL(r)[:, k2 * 128:(k2 + 1) * 128],
                              XH(qh * 4 + k2, off, n), psv(4 + ti, n)) for k2 in range(4)],
                            reads=[tag + "us%d" % r] + [XHN(qh * 4 + k2) for k2 in range(4)],
                            writes=pnames(4 + ti, n), start=(qh == 0), stop=(qh == 3))
                for ti, (off, n) in enumerate(tcsl):
                    act(lambda e, ti=ti, n=n: e.activation(F_SG[:, 0:n], psv(ti, n), AF.Silu),
                        reads=pnames(ti, n), writes=[tag + "sg"])
                    dve(lambda e, ti=ti, n=n: e.tensor_tensor(F_SG[:, 0:n], F_SG[:, 0:n],
                                                              psv(4 + ti, n), op=ALU.mult),
                        reads=pnames(4 + ti, n) + [tag + "sg"], writes=[tag + "sg"])
                    dve(lambda e, mc=mc, off=off, n=n: e.tensor_tensor(
                        FA(mc)[:, off:off + n], F_SG[:, 0:n], F_CWB[:, off:off + n],
                        op=ALU.mult),
                        reads=[tag + "sg", tag + "cwb"], writes=[tag + "a%d" % mc])
            for oc in range(16):
                dma("pool", F_DSL.rearrange("p (mc o) -> p mc o", mc=MCN), wd_b[ex, oc],
                    reads=["stg.wd"], writes=[tag + "ds"])
                for ti, (off, n) in enumerate(tcsl):
                    pg = 4 + (ti % 2)
                    mmg([(F_DSL[:, mc * 128:(mc + 1) * 128], FA(mc)[:, off:off + n],
                          psv(pg, n)) for mc in range(MCN)],
                        reads=[tag + "ds"] + [tag + "a%d" % mc for mc in range(MCN)],
                        writes=pnames(pg, n))
                    dve(lambda e, oc=oc, off=off, n=n, pg=pg: e.tensor_tensor(
                        XC(oc, off, n), XC(oc, off, n), psv(pg, n), op=ALU.add),
                        reads=pnames(pg, n) + [XN(oc)], writes=[XN(oc)])

    ln_main("F1.", TCS, True)
    gate_and_moe("F1.", TP, TCS, T128)
    if "x:ffn1" in taps:
        for fc in range(FC):
            dma("sync", tap_outs["x:ffn1"].rearrange("(fc p) t -> p fc t", p=128)[:, fc, :],
                XC(fc), reads=[XN(fc)], writes=["tapf1"])
    P.fence()

    ln_main("L3.", TCS, False)
    P.fence()

    G_KB = lambda g: ovb(g * 3080, 1540)
    G_VB = ovb(12320, 6708)
    G_PSL = lambda r: ovb(25736 + r * 4096, 2048)
    G_KRAW = ovb(33928, 512)
    G_KTM = ovb(34952, 512)
    G_TC = lambda r: ovb(35976 + r * 512, 256)
    G_TS = lambda r: ovb(37000 + r * 512, 256)
    G_BK = ovb(38024, 512)
    G_BV = ovb(39048, 512)
    G_PR1 = ovb(40072, 256)
    G_PR2 = ovb(40584, 256)
    G_NSC = ovb(41096, 512)

    pool(lambda e: e.memset(G_VB, 1.0), writes=["G.vb"])
    dma("sync", ovf(LN_SCR, 512)[0:1, :], bkb, writes=["G.bkr"])
    mmg([(ONESR, ovf(LN_SCR, 512)[0:1, :], psv(0, 512))], reads=["onesr", "G.bkr"],
        writes=pnames(0, 512))
    act(lambda e: e.copy(G_BK, psv(0, 512)), reads=pnames(0, 512), writes=["G.bk"])
    dma("sync", ovf(LN_SCR, 512)[0:1, :], bvb, writes=["G.bkr"])
    mmg([(ONESR, ovf(LN_SCR, 512)[0:1, :], psv(0, 512))], reads=["onesr", "G.bkr"],
        writes=pnames(0, 512))
    act(lambda e: e.copy(G_BV, psv(0, 512)), reads=pnames(0, 512), writes=["G.bv"])

    IGRP = [list(range(0, 6)), list(range(6, 12)), [12]]
    for mat, wsrc, stgn, is_k in (("k", wkb_b, "stg.wkb", True),
                                  ("v", wvb_b, "stg.wvb", False)):
        for grp in IGRP:
            for qh in range(4):
                r = qh % 2
                dma("sync", G_PSL(r).rearrange("p (ec o) -> p ec o", ec=4),
                    wsrc[qh * 4:(qh + 1) * 4, :, :].rearrange("ec p o -> p ec o"),
                    reads=[stgn], writes=["G.psl%d" % r])
                for gi, i in enumerate(grp):
                    off, n = T128[i]
                    mmg([(XH(qh * 4 + k2, off, n), G_PSL(r)[:, k2 * 512:(k2 + 1) * 512],
                          psv(gi, 512)[0:n, :]) for k2 in range(4)],
                        reads=["G.psl%d" % r] + [XHN(qh * 4 + k2) for k2 in range(4)],
                        writes=pnames(gi, 512), start=(qh == 0), stop=(qh == 3))
            for gi, i in enumerate(grp):
                off, n = T128[i]
                if is_k:
                    dve(lambda e, gi=gi, n=n: e.tensor_tensor(
                        G_KRAW[0:n, :], psv(gi, 512)[0:n, :], G_BK[0:n, :], op=ALU.add),
                        reads=pnames(gi, 512) + ["G.bk"], writes=["G.kraw"])
                    norm_acc("G", G_KRAW[0:n, :], G_NSC[0:n, :], NKB, first=(i == 0),
                             rd=["G.kraw"], n=n)
                    ir = i % 2
                    dma("act", G_TC(ir)[0:n, :], cb_pr[off:off + n, :],
                        writes=["G.tc%d" % ir])
                    dma("act", G_TS(ir)[0:n, :], sb_pr[off:off + n, :],
                        writes=["G.ts%d" % ir])
                    v3 = G_KRAW[0:n, :].rearrange("p (g half f) -> p g half f", g=4, half=2)
                    ko = G_KTM[0:n, :].rearrange("p (g half f) -> p g half f", g=4, half=2)
                    c3 = G_TC(ir)[0:n, :].rearrange("p (g f) -> p g f", g=4)
                    s3 = G_TS(ir)[0:n, :].rearrange("p (g f) -> p g f", g=4)
                    t1 = G_PR1[0:n, :].rearrange("p (g f) -> p g f", g=4)
                    t2 = G_PR2[0:n, :].rearrange("p (g f) -> p g f", g=4)
                    rope6("G.r", v3[:, :, 0, :], v3[:, :, 1, :], c3, s3,
                          ko[:, :, 0, :], ko[:, :, 1, :],
                          rd=["G.kraw", "G.tc%d" % ir, "G.ts%d" % ir], wr=["G.ktm"],
                          t1=t1, t2=t2)
                    for g in range(4):
                        pe(lambda e, g=g, n=n: e.transpose(
                            psbf(6, 1024)[:, g * 256:g * 256 + n],
                            G_KTM[0:n, g * 128:(g + 1) * 128], IDENT_BF[0:n, 0:n]),
                           reads=["G.ktm", "identb"], writes=["ps6"])
                        act(lambda e, g=g, off=off, n=n: e.copy(
                            G_KB(g)[:, off:off + n],
                            psbf(6, 1024)[:, g * 256:g * 256 + n]),
                            reads=["ps6"], writes=["G.kb%d" % g])
                else:
                    dve(lambda e, gi=gi, n=n, i=i: e.tensor_tensor(
                        G_VB[0:n, i * 516:(i + 1) * 516]
                        .rearrange("p (g d) -> p g d", d=129)[:, :, 0:128],
                        psv(gi, 512)[0:n, :].rearrange("p (g d) -> p g d", d=128),
                        G_BV[0:n, :].rearrange("p (g d) -> p g d", d=128), op=ALU.add),
                        reads=pnames(gi, 512) + ["G.bv", "G.vb"], writes=["G.vb"])
    finalize_bound("G", NKB, MAXK2B, ovf(LN_SCR, 128), "maxk2b")
    P.fence()

    G_QSL = lambda r: ovb(25736 + r * 4096, 2048)
    G_MSK = lambda r: ovb(33928 + r * 1024, 512)
    G_PT = lambda r: ovb(35976 + r * 1024, 512)
    G_QRAW = lambda r: ovb(38024 + r * 256, 128)
    G_TC2 = lambda r: ovb(38536 + r * 512, 256)
    G_TS2 = lambda r: ovb(39560 + r * 512, 256)
    G_BQ = ovb(40584, 128)
    G_QH = lambda r: ovb(40840 + r * 2048, 1024)
    G_AO = lambda r: ovb(44936 + r * 512, 256)
    G_ON = ovf(45960, 128)
    G_AR1 = ovb(46472, 128)
    G_AR2 = ovb(46728, 128)
    G_NSQ = ovb(46984, 128)

    BLKB = [(0, 512), (512, 512)]
    for h in range(H):
        hr = h % 2
        g = h // 4
        dma("sync", ovf(LN_SCR + 8192, 128)[0:1, :], bqb[0:1, h * 128:(h + 1) * 128],
            writes=["G.bqr"])
        mmg([(ONESR, ovf(LN_SCR + 8192, 128)[0:1, :], psv(6, 128))],
            reads=["onesr", "G.bqr"], writes=pnames(6, 128))
        act(lambda e: e.copy(G_BQ, psv(6, 128)), reads=pnames(6, 128), writes=["G.bq"])
        for i, (off, n) in enumerate(T128_OWN):
            pk = i % 2
            ir = i % 2
            for qh in range(4):
                r = qh % 2
                dma("sync", G_QSL(r)[:, 0:512].rearrange("p (ec o) -> p ec o", ec=4),
                    wqb_b[qh * 4:(qh + 1) * 4, :, h * 128:(h + 1) * 128]
                    .rearrange("ec p o -> p ec o"),
                    reads=["stg.wqb"], writes=["G.qsl%d" % r])
                mmg([(XH(qh * 4 + k2, off, n), G_QSL(r)[:, k2 * 128:(k2 + 1) * 128],
                      psv(pk, 128)[0:n, :]) for k2 in range(4)],
                    reads=["G.qsl%d" % r] + [XHN(qh * 4 + k2) for k2 in range(4)],
                    writes=pnames(pk, 128), start=(qh == 0), stop=(qh == 3))
            dve(lambda e, pk=pk, n=n, ir=ir: e.tensor_tensor(
                G_QRAW(ir)[0:n, :], psv(pk, 128)[0:n, :], G_BQ[0:n, :], op=ALU.add),
                reads=pnames(pk, 128) + ["G.bq"], writes=["G.qraw%d" % ir])
            norm_acc("Gq", G_QRAW(ir)[0:n, :], G_NSQ[0:n, :], NQA, first=(i == 0),
                     rd=["G.qraw%d" % ir], n=n)
            dma("act", G_TC2(ir)[0:n, :], cb_pr[off:off + n, :], writes=["G.tc2%d" % ir])
            dma("act", G_TS2(ir)[0:n, :], sb_pr[off:off + n, :], writes=["G.ts2%d" % ir])
            v3 = G_QRAW(ir)[0:n, :].rearrange("p (half f) -> p half f", half=2)
            ko = G_AR1[0:n, :].rearrange("p (half f) -> p half f", half=2)
            c3 = G_TC2(ir)[0:n, 0:64]
            s3 = G_TS2(ir)[0:n, 0:64]
            rope6("Gq.r", v3[:, 0, :], v3[:, 1, :], c3, s3,
                  ko[:, 0, :], ko[:, 1, :],
                  rd=["G.qraw%d" % ir, "G.tc2%d" % ir, "G.ts2%d" % ir], wr=["G.qk"],
                  t1=G_AR2[0:n, 0:64], t2=G_AR2[0:n, 64:128])
            pe(lambda e, n=n: e.transpose(psbf(2, 256)[:, 0:n], G_AR1[0:n, :],
                                          IDENT_BF[0:n, 0:n]),
               reads=["G.qk", "identb"], writes=["ps2"])
            act(lambda e, off=off, n=n, hr=hr: e.copy(G_QH(hr)[:, off:off + n],
                                                      psbf(2, 256)[:, 0:n]),
                reads=["ps2"], writes=["G.qh%d" % hr])
        finalize_bound("Gq", NQA, T0, ovf(LN_SCR, 128), "Gq.max2")
        dve(lambda e: e.tensor_tensor(T0, T0, MAXK2B, op=ALU.mult),
            reads=["Gq.max2", "maxk2b"], writes=["G.b"])
        act(lambda e: e.activation(T0, T0, AF.Sqrt), reads=["G.b"], writes=["G.b"])
        act(lambda e: e.activation(T0, T0, AF.Copy, scale=-SCALE_B),
            reads=["G.b"], writes=["G.b"])
        mmg([(ONESR, T0, psv(6, 1))], reads=["onesr", "G.b"], writes=pnames(6, 1))
        act(lambda e, h=h: e.copy(BB(h), psv(6, 1)), reads=pnames(6, 1),
            writes=["Bb%d" % h])
        for bi, (qoff, qn) in enumerate(BLKB):
            for tkc in range(13):
                nk = 128 if tkc < 12 else 4
                sr = tkc % 2
                s_ps = psv(2 + sr, 512)[0:nk, :]
                mmg([(G_KB(g)[:, tkc * 128:tkc * 128 + nk],
                      G_QH(hr)[:, qoff:qoff + qn], s_ps)],
                    reads=["G.kb%d" % g, "G.qh%d" % hr], writes=pnames(2 + sr, 512))
                act(lambda e, sr=sr, s_ps=s_ps, h=h, nk=nk: e.activation(
                    G_PT(sr)[0:nk, :], s_ps, AF.Exp, bias=BB(h)[0:nk, :], scale=SCALE_B),
                    reads=pnames(2 + sr, 512) + ["Bb%d" % h], writes=["G.pt%d" % sr])
                dma("act", G_MSK(sr)[0:nk, :], maskb[bi, tkc, 0:nk, :],
                    writes=["G.mk%d" % sr])
                dve(lambda e, sr=sr, nk=nk: e.tensor_tensor(
                    G_PT(sr)[0:nk, :], G_PT(sr)[0:nk, :], G_MSK(sr)[0:nk, :], op=ALU.mult),
                    reads=["G.mk%d" % sr, "G.pt%d" % sr], writes=["G.pt%d" % sr])

                def pvfn(t, sr=sr, tkc=tkc, nk=nk, g=g):
                    last = None
                    for qi in range(4):
                        last = t.matmul(
                            c_otile(0, qi),
                            G_PT(sr)[0:nk, qi * 128:(qi + 1) * 128],
                            G_VB[0:nk, tkc * 516 + g * 129: tkc * 516 + (g + 1) * 129],
                            start=(tkc == 0 and qi in (0, 3)),
                            stop=(tkc == 12 and qi in (2, 3)))
                    return last
                P.add("pe", pvfn, reads=["G.pt%d" % sr, "G.vb"], writes=pnames(4, 1024))
            for qi in range(4):
                o1 = c_otile(0, qi)
                dve(lambda e, o1=o1: e.reciprocal(EP1, o1[:, 128:129]),
                    reads=pnames(4, 1024), writes=["G.e1"])
                ar = qi % 2
                dve(lambda e, o1=o1, ar=ar: e.tensor_scalar(
                    G_AO(ar)[:, 0:128], o1[:, 0:128], EP1, None, op0=ALU.mult),
                    reads=pnames(4, 1024) + ["G.e1"], writes=["G.ao%d" % ar])
                pe(lambda e, ar=ar: e.transpose(psbf(6, 128), G_AO(ar)[:, 0:128],
                                                IDENT_BF),
                   reads=["G.ao%d" % ar, "identb"], writes=["ps6"])
                act(lambda e, ar=ar: e.copy(G_AO(ar)[:, 128:256], psbf(6, 128)),
                    reads=["ps6"], writes=["G.aot%d" % ar])
                dma("act", afm[h * 128:(h + 1) * 128,
                               qoff + qi * 128: qoff + (qi + 1) * 128],
                    G_AO(ar)[:, 128:256],
                    reads=["G.aot%d" % ar], writes=["afm"])
    P.fence()

    for fc in range(FC):
        dma("sync", XH(fc, 0, TOWN), afm3[:, fc, 0:TOWN], reads=["afm"], writes=[XHN(fc)])
    dma("sync", sm[:, 432:448], bob, writes=["bobc"])
    for oc in range(16):
        orr = oc % 2
        dma("act", C_WQS(orr).rearrange("p (fc o) -> p fc o", fc=FC), wob_b[oc],
            reads=["stg.wob"], writes=["Db.ws%d" % orr])
        for bi, (off, n) in enumerate(TCS_OWN):
            pk = bi % 2
            mmg([(C_WQS(orr)[:, fc * 128:(fc + 1) * 128], XH(fc, off, n),
                  psv(2 * pk, n)) for fc in range(FC)],
                reads=["Db.ws%d" % orr] + [XHN(fc) for fc in range(FC)],
                writes=pnames(2 * pk, n))
            dve(lambda e, oc=oc, off=off, n=n, pk=pk: e.scalar_tensor_tensor(
                XC(oc, off, n), psv(2 * pk, n), sm[:, 432 + oc:433 + oc], XC(oc, off, n),
                op0=ALU.add, op1=ALU.add),
                reads=pnames(2 * pk, n) + [XN(oc), "bobc"], writes=[XN(oc)])
    if "x:attn_b" in taps:
        for fc in range(FC):
            dma("sync", tap_outs["x:attn_b"].rearrange("(fc p) t -> p fc t", p=128)[:, fc, :],
                XC(fc), reads=[XN(fc)], writes=["tapxb"])
    P.fence()

    ln_main("F2.", TCS_OWN, True)
    gate_and_moe("F2.", TOWN, TCS_OWN, T128_OWN)
    P.fence()

    for fc in range(FC):
        dma("sync", y3[:, fc, :], XC(fc, 0, TOWN), reads=[XN(fc)], writes=["y"])
    P.add("sync", lambda e: e.nop(), reads=["y"])

    P.emit(block, engsem, dmasem)
    es.close()
    return nc



_PREPPED = {}


def _prep(inputs):
    key = id(inputs.get("L0_attn_a_x"))
    x = np.asarray(inputs["L0_attn_a_x"], np.float32)
    wqa = _rhsT3(np.asarray(inputs["L0_attn_a_Wq"], np.float32))
    wka = _rhsT3(np.asarray(inputs["L0_attn_a_Wk"], np.float32))
    wva = _rhsT3(np.asarray(inputs["L0_attn_a_Wv"], np.float32))
    woa = _oslab(np.asarray(inputs["L0_attn_a_Wo"], np.float32))
    wqb = _rhsT3_em(np.asarray(inputs["L0_attn_b_Wq"], np.float32))
    wkb = _rhsT3_em(np.asarray(inputs["L0_attn_b_Wk"], np.float32))
    wvb = _rhsT3_em(np.asarray(inputs["L0_attn_b_Wv"], np.float32))
    wob = _oslab_em(np.asarray(inputs["L0_attn_b_Wo"], np.float32))
    gate_sl = np.ascontiguousarray(
        np.asarray(inputs["L0_ffn_gate_w"], np.float32).T.reshape(FC, 128, NEXP))
    wg = _guslab(np.asarray(inputs["L0_ffn_w_gate"], np.float32))
    wu = _guslab(np.asarray(inputs["L0_ffn_w_up"], np.float32))
    wd = _dslab(np.asarray(inputs["L0_ffn_w_down"], np.float32))

    com = {
        "wqa": wqa, "wka": wka, "wva": wva, "woa": woa,
        "wqb": wqb, "wkb": wkb, "wvb": wvb, "wob": wob,
        "bqb": np.asarray(inputs["L0_attn_b_bq"], np.float32).reshape(1, E),
        "bkb": np.asarray(inputs["L0_attn_b_bk"], np.float32).reshape(1, 512),
        "bvb": np.asarray(inputs["L0_attn_b_bv"], np.float32).reshape(1, 512),
        "bob": np.ascontiguousarray(
            np.asarray(inputs["L0_attn_b_bo"], np.float32).reshape(16, 128).T),
        "gate_sl": gate_sl, "wg": wg, "wu": wu, "wd": wd,
        "lq1": np.asarray(inputs["L0_attn_a_lambda_q1"], np.float32).reshape(1, 64),
        "lk1": np.asarray(inputs["L0_attn_a_lambda_k1"], np.float32).reshape(1, 64),
        "lq2": np.asarray(inputs["L0_attn_a_lambda_q2"], np.float32).reshape(1, 64),
        "lk2": np.asarray(inputs["L0_attn_a_lambda_k2"], np.float32).reshape(1, 64),
        "gam": np.asarray(inputs["L0_attn_a_gamma"], np.float32).reshape(1, 128),
        "ident": np.eye(128, dtype=np.float32),
        "identb": np.eye(128, dtype=np.float32).astype(ml_dtypes.bfloat16),
    }
    seq_pos = np.arange(TSEQ)
    ca_seq, sa_seq = _rope_a_tables(seq_pos)
    BF = ml_dtypes.bfloat16
    com["ca_seq"], com["sa_seq"] = ca_seq.astype(BF), sa_seq.astype(BF)

    in_maps = []
    for core in range(8):
        s, j = core // 4, core % 4
        xT = np.ascontiguousarray(x[s].T)
        own = xT[:, j * TOWN:(j + 1) * TOWN]
        ext = (xT[:, j * TOWN - TEXT: j * TOWN] if j > 0
               else np.zeros((E, TEXT), np.float32))
        sink = xT[:, 0:TSINK]
        x_proc = np.ascontiguousarray(np.concatenate([own, ext, sink], axis=1))
        pos = np.concatenate([
            j * TOWN + np.arange(TOWN),
            (j * TOWN - TEXT + np.arange(TEXT)) if j > 0 else np.zeros(TEXT, np.int64),
            np.arange(TSINK)])
        ca_pr, sa_pr = _rope_a_tables(pos)
        cb_pr, sb_pr = _rope_b_tables(pos)
        ca_pr, sa_pr = ca_pr.astype(BF), sa_pr.astype(BF)
        cb_pr, sb_pr = cb_pr.astype(BF), sb_pr.astype(BF)
        m = dict(com)
        m["x_fm"] = np.ascontiguousarray(xT)
        m["x_proc"] = x_proc
        m["ca_pr"], m["sa_pr"] = ca_pr, sa_pr
        m["cb_pr"], m["sb_pr"] = cb_pr, sb_pr
        m["maskb"] = _attnb_masks(j).astype(BF)
        in_maps.append(m)
    return in_maps


def kernel(**inputs):
    in_maps = _prep(inputs)
    nc = build()
    res = run_bass_kernel_spmd(nc, in_maps, core_ids=list(range(8)))
    out = np.empty((2, TSEQ, E), np.float32)
    for core in range(8):
        s, j = core // 4, core % 4
        out[s, j * TOWN:(j + 1) * TOWN, :] = res.results[core]["y"].T
    return out


# version 04 (final)
# speedup vs baseline: 1.6710x; 1.6710x over previous
import math
import contextlib
import numpy as np
import ml_dtypes

import concourse.bass as bass
import concourse.mybir as mybir
from concourse.bass_utils import run_bass_kernel_spmd

F32 = mybir.dt.float32
BF16 = mybir.dt.bfloat16
AF = mybir.ActivationFunctionType
ALU = mybir.AluOpType
AX = mybir.AxisListType

E = 2048
FC = 16
TSEQ = 4096
TOWN = 1024
TEXT = 512
TSINK = 4
TP = TOWN + TEXT + TSINK
H = 16
MCN = 11
NEXP = 16
TOPK = 6
LAMBDA_INIT = 0.8 - 0.6 * math.exp(-0.3 * 12)
SCALE_A = 64.0 ** -0.5
SCALE_B = 128.0 ** -0.5
EPS = 1e-5
WINDOW = 512
SINKN = 4


def fchunks(total, step=512):
    out, o = [], 0
    while o < total:
        n = min(step, total - o)
        out.append((o, n))
        o += n
    return out


TCS = fchunks(TP)
TCS_OWN = fchunks(TOWN)
T128 = fchunks(TP, 128)
T128_OWN = fchunks(TOWN, 128)


NLANES = 8


class _Op:
    __slots__ = ("idx", "eng", "fn", "deps", "sem", "val", "dma", "dmaidx",
                 "dbg_r", "dbg_w")


class Prog:
    ENG = ("pe", "act", "dve", "pool", "sync")
    QUEUES = ("sync", "act", "pool")

    def __init__(self):
        self.ops = []
        self.res = {}
        self.fence_deps = set()
        self.dcnt_add = {q: 0 for q in self.QUEUES}

    def add(self, eng, fn, reads=(), writes=(), dma=False):
        deps = set(self.fence_deps)
        for r in reads:
            e = self.res.get(r)
            if e is not None and e[0] is not None:
                deps.add(e[0])
        for w in writes:
            e = self.res.get(w)
            if e is not None:
                if e[0] is not None:
                    deps.add(e[0])
                deps |= e[1]
        i = len(self.ops)
        op = _Op()
        op.idx, op.eng, op.fn, op.dma, op.sem = i, eng, fn, dma, None
        if dma:
            op.dmaidx = self.dcnt_add[eng]
            self.dcnt_add[eng] += 1
        op.deps = {d for d in deps if d != i}
        op.dbg_r, op.dbg_w = list(reads), list(writes)
        self.ops.append(op)
        for w in writes:
            self.res[w] = [i, set()]
        for r in reads:
            e = self.res.setdefault(r, [None, set()])
            e[1].add(i)
        return i

    def fence(self):
        last = {}
        for op in self.ops:
            key = (op.eng, op.dmaidx % NLANES) if op.dma else (op.eng, None)
            last[key] = op.idx
        self.fence_deps = set(last.values())

    def emit(self, block, engsem, dmasem):
        needed = set()
        for op in self.ops:
            needed |= op.deps
        cnt = {e: 0 for e in self.ENG}
        for op in self.ops:
            if op.dma:
                op.sem = dmasem[op.eng][op.dmaidx % NLANES]
                op.val = 16 * (op.dmaidx // NLANES + 1)
            elif op.idx in needed:
                cnt[op.eng] += 1
                op.sem, op.val = engsem[op.eng], cnt[op.eng]
        streams = {e: [] for e in self.ENG}
        for op in self.ops:
            streams[op.eng].append(op)
        allops = self.ops

        def runner(name):
            def run(eng):
                sat = {}
                for op in streams[name]:
                    waits = {}
                    for d in op.deps:
                        dop = allops[d]
                        k = id(dop.sem)
                        if sat.get(k, 0) >= dop.val:
                            continue
                        prev = waits.get(k)
                        waits[k] = (dop.sem,
                                    dop.val if prev is None else max(prev[1], dop.val))
                    if op.dma and op.val > 16:
                        k = id(op.sem)
                        pv = op.val - 16
                        if sat.get(k, 0) < pv and not (k in waits and waits[k][1] >= pv):
                            prev = waits.get(k)
                            waits[k] = (op.sem,
                                        pv if prev is None else max(prev[1], pv))
                    for k, (s, v) in waits.items():
                        eng.wait_ge(s, v)
                        sat[k] = v
                    try:
                        inst = op.fn(eng)
                    except Exception as ex:
                        raise RuntimeError(
                            "op %d on %s failed (reads=%s writes=%s): %s"
                            % (op.idx, name, getattr(op, "dbg_r", "?"),
                               getattr(op, "dbg_w", "?"), ex)) from ex
                    if op.sem is not None:
                        inst.then_inc(op.sem, 16 if op.dma else 1)
            return run

        block.tensor(runner("pe"))
        block.scalar(runner("act"))
        block.vector(runner("dve"))
        block.gpsimd(runner("pool"))
        block.sync(runner("sync"))



def _rhsT3(W):
    return np.ascontiguousarray(W.T.reshape(FC, 128, W.shape[0]))

def _rhsT3_em(W):
    return np.ascontiguousarray(W.reshape(FC, 128, W.shape[1]))

def _oslab(W):
    return np.ascontiguousarray(W.reshape(16, 128, 16, 128).transpose(0, 3, 2, 1))

def _oslab_em(W):
    return np.ascontiguousarray(W.reshape(16, 128, 16, 128).transpose(2, 1, 0, 3))

def _guslab(W):
    return np.ascontiguousarray(W.reshape(NEXP, MCN, 128, FC, 128).transpose(0, 1, 4, 3, 2))

def _dslab(W):
    return np.ascontiguousarray(W.reshape(NEXP, 16, 128, MCN, 128).transpose(0, 1, 4, 3, 2))


def _rope_a_tables(pos):
    i = np.arange(32, dtype=np.float64)
    ang = 1.0 / 10000.0 ** (i / 31.0)
    th = pos.astype(np.float64)[:, None] * ang[None, :]
    return (np.tile(np.cos(th), (1, 8)).astype(np.float32),
            np.tile(np.sin(th), (1, 8)).astype(np.float32))


def _rope_b_tables(pos):
    j = np.arange(64, dtype=np.float64)
    inv = 1.0 / 10000.0 ** (j / 64.0)
    th = pos.astype(np.float64)[:, None] * inv[None, :]
    return (np.tile(np.cos(th), (1, 4)).astype(np.float32),
            np.tile(np.sin(th), (1, 4)).astype(np.float32))


def _attnb_masks(j):
    base = j * TOWN
    lg = np.full(TP, -10 ** 9, dtype=np.int64)
    lg[0:TOWN] = base + np.arange(TOWN)
    if j > 0:
        lg[TOWN:TOWN + TEXT] = base - TEXT + np.arange(TEXT)
    lg[TOWN + TEXT:] = np.arange(TSINK)
    lo = base - TEXT if j > 0 else base
    canon = np.zeros(TP, dtype=bool)
    canon[0:TOWN + TEXT] = lg[0:TOWN + TEXT] >= 0
    for m in range(TSINK):
        if not (lo <= m < base + TOWN):
            canon[TOWN + TEXT + m] = True
    gq = base + np.arange(TOWN)
    masks = np.zeros((2, 13, 128, 512), dtype=np.float32)
    for qb in range(2):
        q = gq[qb * 512:(qb + 1) * 512]
        for kc in range(13):
            p0 = kc * 128
            npart = min(128, TP - p0)
            k = lg[p0:p0 + npart]
            ok = canon[p0:p0 + npart]
            m = (k[:, None] <= q[None, :]) & (
                (k[:, None] >= q[None, :] - WINDOW) | (k[:, None] < SINKN)) & ok[:, None]
            masks[qb, kc, :npart, :] = m.astype(np.float32)
    return masks



OVB = 57344


def build(taps=()):
    nc = bass.Bass("TRN2", target_bir_lowering=False)
    es = contextlib.ExitStack()

    def din(name, shape, dt=F32):
        return nc.dram_tensor(name, list(shape), dt, kind="ExternalInput").ap()

    x_fm = din("x_fm", (E, TSEQ))
    x_proc = din("x_proc", (E, TP))
    wqa = din("wqa", (FC, 128, E))
    wka = din("wka", (FC, 128, E))
    wva = din("wva", (FC, 128, E))
    woa = din("woa", (16, 128, 16, 128))
    wqb = din("wqb", (FC, 128, E))
    wkb = din("wkb", (FC, 128, 512))
    wvb = din("wvb", (FC, 128, 512))
    wob = din("wob", (16, 128, 16, 128))
    bqb = din("bqb", (1, E))
    bkb = din("bkb", (1, 512))
    bvb = din("bvb", (1, 512))
    bob = din("bob", (128, 16))
    gate_sl = din("gate_sl", (FC, 128, NEXP))
    wg = din("wg", (NEXP, MCN, 128, FC, 128))
    wu = din("wu", (NEXP, MCN, 128, FC, 128))
    wd = din("wd", (NEXP, 16, 128, MCN, 128))
    lq1 = din("lq1", (1, 64)); lk1 = din("lk1", (1, 64))
    lq2 = din("lq2", (1, 64)); lk2 = din("lk2", (1, 64))
    gam = din("gam", (1, 128))
    ident = din("ident", (128, 128))
    identb = din("identb", (128, 128), BF16)
    ca_seq = din("ca_seq", (TSEQ, 256), BF16); sa_seq = din("sa_seq", (TSEQ, 256), BF16)
    ca_pr = din("ca_pr", (TP, 256), BF16); sa_pr = din("sa_pr", (TP, 256), BF16)
    cb_pr = din("cb_pr", (TP, 256), BF16); sb_pr = din("sb_pr", (TP, 256), BF16)
    maskb = din("maskb", (2, 13, 128, 512), BF16)

    y = nc.dram_tensor("y", [E, TOWN], F32, kind="ExternalOutput").ap()
    kvk = nc.dram_tensor("kvk", [E, TSEQ], BF16,
                         kind="ExternalOutput" if "kvk" in taps else "Internal").ap()
    kvv = nc.dram_tensor("kvv", [H, 32, 128, 128], BF16,
                         kind="ExternalOutput" if "kvv" in taps else "Internal").ap()
    afm = nc.dram_tensor("afm", [E, TP], BF16,
                         kind="ExternalOutput" if "afm" in taps else "Internal").ap()
    tap_outs = {}
    for t in taps:
        if t.startswith("x:"):
            tap_outs[t] = nc.dram_tensor("tap_" + t[2:], [E, TP], F32,
                                         kind="ExternalOutput").ap()

    sbt = lambda name, shape, dt: es.enter_context(nc.sbuf_tensor(name, list(shape), dt))
    xres = sbt("xres", (128, FC * TP), F32)
    xhat = sbt("xhat", (128, FC * TP), BF16)
    ov = sbt("ov", (128, OVB), mybir.dt.uint8)
    sm = sbt("sm", (128, 512), F32)
    smb = sbt("smb", (128, 272), BF16)
    gts = sbt("gts", (128, FC * NEXP), F32)
    ps = es.enter_context(nc.psum_tensor("ps", [128, 4096], F32))

    engsem = {e: es.enter_context(nc.semaphore("sem_" + e)) for e in Prog.ENG}
    dmasem = {q: [es.enter_context(nc.semaphore("dsem_%s_%d" % (q, l)))
                  for l in range(NLANES)] for q in Prog.QUEUES}
    block = es.enter_context(nc.Block())

    def ovf(byte, n):
        assert byte % 4 == 0 and byte + 4 * n <= OVB, (byte, n)
        return ov[:, byte: byte + 4 * n].bitcast(F32)

    def ovb(byte, n):
        assert byte % 2 == 0 and byte + 2 * n <= OVB, (byte, n)
        return ov[:, byte: byte + 2 * n].bitcast(BF16)

    def psv(bank, words):
        assert bank * 512 + words <= 4096
        return ps[:, bank * 512: bank * 512 + words]

    def psbf(bank, n):
        return ps[:, bank * 512: bank * 512 + (n + 1) // 2].bitcast(BF16)[:, 0:n]

    def pnames(bank, words):
        return ["ps%d" % b for b in range(bank, bank + (words + 511) // 512)]

    ONESR = sm[0:1, 0:128]
    ONESC = sm[:, 128:129]
    GAMN = sm[:, 129:257]
    LAMC = sm[:, 257:258]
    BA = lambda h: sm[:, 258 + h:259 + h]
    BB = lambda h: sm[:, 274 + h:275 + h]
    NKA = sm[:, 290:291]
    NKB = sm[:, 291:292]
    NQA = sm[:, 292:293]
    NQT = sm[:, 293:294]
    T0 = sm[0:1, 294:295]
    T1 = sm[0:1, 295:296]
    MAXK2A = sm[0:1, 296:297]
    MAXK2B = sm[0:1, 297:298]
    COLG = sm[0:16, 298:299]
    EP1 = sm[:, 299:300]
    NEGBIG = sm[:, 302:303]
    EPSC = sm[:, 303:304]
    LAMI = sm[:, 432:433]
    LAMI = sm[:, 498:499]
    EP2 = sm[:, 300:301]
    EP3 = sm[:, 301:302]
    IDENT_F = sm[:, 304:432]
    ROW64 = sm[0:1, 432:496]
    ROW64B = sm[0:1, 496:512]
    ONESC_BF = smb[:, 0:1]
    ONESR_BF = smb[0:1, 8:136]
    IDENT_BF = smb[:, 136:264]

    P = Prog()

    def dma(q, out_ap, in_ap, reads=(), writes=()):
        return P.add(q, lambda e, o=out_ap, i=in_ap: e.dma_start(out=o, in_=i),
                     reads=reads, writes=writes, dma=True)

    def mmg(pairs, reads=(), writes=(), start=True, stop=True, each_start=False):
        def fn(t, pairs=pairs, start=start, stop=stop, each_start=each_start):
            last, n = None, len(pairs)
            for k, (l, r, o) in enumerate(pairs):
                last = t.matmul(o, l, r,
                                start=(True if each_start else (start and k == 0)),
                                stop=(True if each_start else (stop and k == n - 1)))
            return last
        return P.add("pe", fn, reads=reads, writes=writes)

    def mmgf(pairs, reads=(), writes=()):
        def fn(t, pairs=pairs):
            last = None
            for (l, r, o, st, sp) in pairs:
                last = t.matmul(o, l, r, start=st, stop=sp)
            return last
        return P.add("pe", fn, reads=reads, writes=writes)

    act = lambda fn, reads=(), writes=(): P.add("act", fn, reads=reads, writes=writes)
    dve = lambda fn, reads=(), writes=(): P.add("dve", fn, reads=reads, writes=writes)
    pool = lambda fn, reads=(), writes=(): P.add("pool", fn, reads=reads, writes=writes)
    pe = lambda fn, reads=(), writes=(): P.add("pe", fn, reads=reads, writes=writes)

    XC = lambda fc, off=0, n=TP: xres[:, fc * TP + off: fc * TP + off + n]
    XH = lambda fc, off=0, n=TP: xhat[:, fc * TP + off: fc * TP + off + n]
    XN = lambda fc: "x%d" % fc
    XHN = lambda fc: "xh%d" % fc

    x_fm3 = x_fm.rearrange("(fc p) t -> p fc t", p=128)
    x_proc3 = x_proc.rearrange("(fc p) t -> p fc t", p=128)
    kvk3 = kvk.rearrange("(fc p) t -> p fc t", p=128)
    afm3 = afm.rearrange("(fc p) t -> p fc t", p=128)
    y3 = y.rearrange("(fc p) t -> p fc t", p=128)

    def ln_fm(tag, xin, xout, rnames, wnames, tcsl, rows, scr, inplace):
        MR_ = ovf(scr, 512)[0:1, :]
        VR_ = ovf(scr + 2048, 512)[0:1, :]
        M2_ = ovf(scr + 4096, 512)[0:1, :]
        XB_ = ovb(scr + 6144, 512)
        XQ_ = ovb(scr + 7168, 512)
        T1_ = None if inplace else ovf(scr + 8192, 512)
        rres, mrres = tag + "R", tag + "MR"
        for (off, n) in tcsl:
            if rows is not None:
                rrow = rows[0][0:1, off:off + n]
                mrrow = rows[1][0:1, off:off + n]
                rxtra, mxtra = [], []
            else:
                rrow = ovf(scr + 4096, n)[0:1, :]
                mrrow = ovf(scr + 6144, n)[0:1, :]
                rxtra, mxtra = [tag + "m2"], [tag + "xb", tag + "xq"]
            s_ps, q_ps = psv(4, n)[0:1, :], psv(5, n)[0:1, :]
            for fc in range(FC):
                dve(lambda e, fc=fc, off=off, n=n: e.tensor_copy(XB_[:, 0:n], xin(fc, off, n)),
                    reads=[rnames(fc)], writes=[tag + "xb"])
                act(lambda e, fc=fc, off=off, n=n: e.activation(XQ_[:, 0:n], xin(fc, off, n),
                                                                AF.Square),
                    reads=[rnames(fc)], writes=[tag + "xq"])
                mmg([(ONESC_BF, XB_[:, 0:n], s_ps)], reads=["onescb", tag + "xb"],
                    writes=pnames(4, n), start=(fc == 0), stop=(fc == FC - 1))
                mmg([(ONESC_BF, XQ_[:, 0:n], q_ps)], reads=["onescb", tag + "xq"],
                    writes=pnames(5, n), start=(fc == 0), stop=(fc == FC - 1))
            act(lambda e, n=n, s_ps=s_ps: e.activation(MR_[:, 0:n], s_ps, AF.Copy,
                                                       scale=1.0 / E),
                reads=pnames(4, n), writes=[tag + "m"])
            act(lambda e, n=n, q_ps=q_ps: e.activation(VR_[:, 0:n], q_ps, AF.Copy,
                                                       scale=1.0 / E),
                reads=pnames(5, n), writes=[tag + "v"])
            act(lambda e, n=n: e.activation(M2_[:, 0:n], MR_[:, 0:n], AF.Square),
                reads=[tag + "m"], writes=[tag + "m2"])
            dve(lambda e, n=n: e.tensor_tensor(VR_[:, 0:n], VR_[:, 0:n], M2_[:, 0:n],
                                               op=ALU.subtract),
                reads=[tag + "v", tag + "m2"], writes=[tag + "v"])
            act(lambda e, n=n: e.activation(VR_[:, 0:n], VR_[:, 0:n], AF.Sqrt,
                                            bias=EPSC[0:1, :]),
                reads=[tag + "v", "epsc"], writes=[tag + "v"])
            dve(lambda e, n=n, rrow=rrow: e.reciprocal(rrow, VR_[:, 0:n]),
                reads=[tag + "v"], writes=[rres] + rxtra)
            dve(lambda e, n=n, rrow=rrow, mrrow=mrrow: e.tensor_tensor(
                mrrow, MR_[:, 0:n], rrow, op=ALU.mult),
                reads=[tag + "m", rres], writes=[mrres] + mxtra)
            r_ps, m_ps = psv(6, n), psv(7, n)
            mmg([(ONESR, rrow, r_ps)], reads=["onesr", rres] + rxtra,
                writes=pnames(6, n))
            mmg([(ONESR, mrrow, m_ps)], reads=["onesr", mrres] + mxtra,
                writes=pnames(7, n))
            for fc in range(FC):
                if inplace:
                    dve(lambda e, fc=fc, off=off, n=n, r_ps=r_ps: e.tensor_tensor(
                        xin(fc, off, n), xin(fc, off, n), r_ps, op=ALU.mult),
                        reads=[rnames(fc)] + pnames(6, n), writes=[rnames(fc)])
                    dve(lambda e, fc=fc, off=off, n=n, m_ps=m_ps: e.tensor_tensor(
                        xout(fc, off, n), xin(fc, off, n), m_ps, op=ALU.subtract),
                        reads=[rnames(fc)] + pnames(7, n), writes=[wnames(fc)])
                else:
                    dve(lambda e, fc=fc, off=off, n=n, r_ps=r_ps: e.tensor_tensor(
                        T1_[:, 0:n], xin(fc, off, n), r_ps, op=ALU.mult),
                        reads=[rnames(fc)] + pnames(6, n), writes=[tag + "t1"])
                    dve(lambda e, fc=fc, off=off, n=n, m_ps=m_ps: e.tensor_tensor(
                        xout(fc, off, n), T1_[:, 0:n], m_ps, op=ALU.subtract),
                        reads=[tag + "t1"] + pnames(7, n), writes=[wnames(fc)])

    LN_SCR = 47104

    def ln_main(tag, tcsl, keep_rows):
        rows = (ovf(0, TP)[0:1, :], ovf(6160, TP)[0:1, :]) if keep_rows else None
        ln_fm(tag, lambda fc, off, n: XC(fc, off, n), lambda fc, off, n: XH(fc, off, n),
              XN, XHN, tcsl, rows, LN_SCR, False)

    def rope6(tag, xe, xo, c3, s3, oe, oo, rd, wr, t1, t2):
        dve(lambda e: e.tensor_tensor(t1, xe, c3, op=ALU.mult), reads=rd, writes=[tag + "1"])
        dve(lambda e: e.tensor_tensor(t2, xo, s3, op=ALU.mult), reads=rd, writes=[tag + "2"])
        dve(lambda e: e.tensor_tensor(oe, t1, t2, op=ALU.subtract),
            reads=[tag + "1", tag + "2"], writes=wr)
        dve(lambda e: e.tensor_tensor(t1, xe, s3, op=ALU.mult), reads=rd, writes=[tag + "1"])
        dve(lambda e: e.tensor_tensor(t2, xo, c3, op=ALU.mult), reads=rd, writes=[tag + "2"])
        dve(lambda e: e.tensor_tensor(oo, t1, t2, op=ALU.add),
            reads=[tag + "1", tag + "2"], writes=wr)

    def norm_acc(tag, src, nscr, col, first, rd, n=128):
        act(lambda e: e.activation(nscr, src, AF.Square, accum_out=NQT[0:n, :]),
            reads=rd, writes=[tag + "nscr", "nqt"])
        if first:
            dve(lambda e: e.tensor_copy(col[0:n, :], NQT[0:n, :]),
                reads=["nqt"], writes=[tag + "ncol"])
        else:
            dve(lambda e: e.tensor_tensor(col[0:n, :], col[0:n, :], NQT[0:n, :], op=ALU.max),
                reads=["nqt", tag + "ncol"], writes=[tag + "ncol"])

    def finalize_bound(tag, col, out_scalar, rowscr, outname):
        pe(lambda e: e.transpose(psv(3, 128)[0:1, :], col, IDENT_F),
           reads=[tag + "ncol", "identf"], writes=pnames(3, 128))
        act(lambda e: e.copy(rowscr[0:1, 0:128], psv(3, 128)[0:1, :]),
            reads=pnames(3, 128), writes=[tag + "nrow"])
        dve(lambda e: e.reduce_max(out_scalar, rowscr[0:1, 0:128], axis=AX.X),
            reads=[tag + "nrow"], writes=[outname])

    pool(lambda e: e.memset(ONESR, 1.0), writes=["onesr"])
    pool(lambda e: e.memset(NEGBIG, -1e5), writes=["negbig"])
    pool(lambda e: e.memset(EPSC, EPS), writes=["epsc"])
    pool(lambda e: e.memset(LAMI, LAMBDA_INIT), writes=["lami"])
    pool(lambda e: e.memset(ONESC, 1.0), writes=["onesc"])
    pool(lambda e: e.memset(ONESC_BF, 1.0), writes=["onescb"])
    pool(lambda e: e.memset(ONESR_BF, 1.0), writes=["onesrb"])
    dma("sync", IDENT_F, ident, writes=["identf"])
    dma("sync", IDENT_BF, identb, writes=["identb"])
    dma("sync", gts[:].rearrange("p (ec o) -> p ec o", ec=FC),
        gate_sl.rearrange("ec p o -> p ec o"), writes=["gts"])

    dma("sync", ovf(0, 128)[0:1, :], gam, writes=["gamrow"])
    mmg([(ONESR, ovf(0, 128)[0:1, :], psv(0, 128))], reads=["onesr", "gamrow"],
        writes=pnames(0, 128))
    act(lambda e: e.activation(GAMN, psv(0, 128), AF.Copy, scale=-(1.0 - LAMBDA_INIT)),
        reads=pnames(0, 128), writes=["gamn"])

    dma("sync", ROW64, lq1, writes=["l1"])
    dma("sync", ovf(512, 64)[0:1, :], lk1, writes=["l2"])
    dma("sync", ovf(768, 64)[0:1, :], lq2, writes=["l3"])
    dma("sync", ovf(1024, 64)[0:1, :], lk2, writes=["l4"])
    dve(lambda e: e.tensor_tensor(ROW64, ROW64, ovf(512, 64)[0:1, :], op=ALU.mult),
        reads=["l1", "l2"], writes=["l1"])
    dve(lambda e: e.tensor_tensor(ovf(768, 64)[0:1, :], ovf(768, 64)[0:1, :],
                                  ovf(1024, 64)[0:1, :], op=ALU.mult),
        reads=["l3", "l4"], writes=["l3"])
    dve(lambda e: e.reduce_sum(T0, ROW64, axis=AX.X), reads=["l1"], writes=["t0"])
    dve(lambda e: e.reduce_sum(T1, ovf(768, 64)[0:1, :], axis=AX.X),
        reads=["l3"], writes=["t1"])
    act(lambda e: e.activation(T0, T0, AF.Exp), reads=["t0"], writes=["t0"])
    act(lambda e: e.activation(T1, T1, AF.Exp), reads=["t1"], writes=["t1"])
    dve(lambda e: e.tensor_tensor(T0, T0, T1, op=ALU.subtract),
        reads=["t0", "t1"], writes=["t0"])
    dve(lambda e: e.tensor_tensor(T0, T0, LAMI[0:1, :], op=ALU.add),
        reads=["t0", "lami"], writes=["t0"])
    mmg([(ONESR, T0, psv(0, 1))], reads=["onesr", "t0"], writes=pnames(0, 1))
    act(lambda e: e.copy(LAMC, psv(0, 1)), reads=pnames(0, 1), writes=["lam"])

    mmg([(gts[:, ec * NEXP:(ec + 1) * NEXP], ONESC, psv(1, 1)[0:16, :]) for ec in range(FC)],
        reads=["gts", "onesc"], writes=pnames(1, 1))
    act(lambda e: e.copy(COLG, psv(1, 1)[0:16, :]), reads=pnames(1, 1), writes=["colg"])

    P.fence()

    A_XT = lambda fc: ovf(fc * 2048, 512)
    A_XH = lambda fc: ovb(32768 + fc * 1024, 512)
    A_KT = lambda i: ovb(i * 4096, 2048)
    A_KRAW = ovb(16384, 512)
    A_RS1 = ovb(17408, 256)
    A_RS2 = ovb(17920, 256)
    A_VST = lambda r: ovb(18432 + r * 1024, 512)
    A_KFM = lambda r: ovb(20480 + r * 1024, 512)
    A_NSC = ovb(22528, 512)
    A_TC = ovb(23552, 256)
    A_TS = ovb(24064, 256)
    A_WS = lambda r: ovb(24576 + r * 4096, 2048)

    def a_ktnames(i):
        return ["A.xt%d" % (2 * i), "A.xt%d" % (2 * i + 1)]

    A_KWORK = (["A.ws0", "A.ws1", "A.kraw", "A.vst0", "A.vst1", "A.kfm0", "A.kfm1",
                "A.nsc", "A.tc", "A.ts", "A.r1", "A.r2"])
    for c in range(8):
        dma("sync", ov[:, 0:32768].bitcast(F32).rearrange("p (fc t) -> p fc t", fc=FC),
            x_fm3[:, :, c * 512:(c + 1) * 512],
            writes=["A.xt%d" % fc for fc in range(FC)] + A_KWORK)
        ln_fm("A.ln", lambda fc, off, n: A_XT(fc)[:, off:off + n],
              lambda fc, off, n: A_XH(fc)[:, off:off + n],
              lambda fc: "A.xt%d" % fc, lambda fc: "A.xh%d" % fc,
              [(0, 512)], None, 49152, True)
        for mat, wsrc, is_k in (("k", wka, True), ("v", wva, False)):
            for o5 in range(4):
                for qh in range(4):
                    r = qh % 2
                    dma("pool", A_WS(r).rearrange("p (ec o) -> p ec o", ec=4),
                        wsrc[qh * 4:(qh + 1) * 4, :, o5 * 512:(o5 + 1) * 512]
                        .rearrange("ec p o -> p ec o"),
                        writes=["A.ws%d" % r,
                                "A.xt%d" % (12 + 2 * r), "A.xt%d" % (13 + 2 * r)])
                    for i in range(4):
                        mmg([(A_XH(qh * 4 + k2)[:, i * 128:(i + 1) * 128],
                              A_WS(r)[:, k2 * 512:(k2 + 1) * 512], psv(i, 512))
                             for k2 in range(4)],
                            reads=["A.ws%d" % r] + ["A.xh%d" % (qh * 4 + k2)
                                                    for k2 in range(4)],
                            writes=pnames(i, 512),
                            start=(qh == 0), stop=(qh == 3))
                if is_k:
                    for i in range(4):
                        act(lambda e, i=i: e.copy(A_KRAW, psv(i, 512)),
                            reads=pnames(i, 512), writes=["A.kraw"])
                        norm_acc("A", A_KRAW, A_NSC, NKA,
                                 first=(c == 0 and o5 == 0 and i == 0), rd=["A.kraw"])
                        dma("act", A_TC[0:128, :],
                            ca_seq[c * 512 + i * 128: c * 512 + (i + 1) * 128, :],
                            writes=["A.tc"])
                        dma("act", A_TS[0:128, :],
                            sa_seq[c * 512 + i * 128: c * 512 + (i + 1) * 128, :],
                            writes=["A.ts"])
                        v4 = A_KRAW.rearrange("p (rh f two) -> p rh f two", rh=8, two=2)
                        ko = A_KT(i)[:, o5 * 512:(o5 + 1) * 512].rearrange(
                            "p (rh f two) -> p rh f two", rh=8, two=2)
                        c3 = A_TC.rearrange("p (rh f) -> p rh f", rh=8)
                        s3 = A_TS.rearrange("p (rh f) -> p rh f", rh=8)
                        t1 = A_RS1.rearrange("p (rh f) -> p rh f", rh=8)
                        t2 = A_RS2.rearrange("p (rh f) -> p rh f", rh=8)
                        rope6("A.r", v4[:, :, :, 0], v4[:, :, :, 1], c3, s3,
                              ko[:, :, :, 0], ko[:, :, :, 1],
                              rd=["A.kraw", "A.tc", "A.ts"], wr=a_ktnames(i),
                              t1=t1, t2=t2)
                else:
                    for i in range(4):
                        vr = i % 2
                        act(lambda e, i=i, vr=vr: e.copy(A_VST(vr), psv(i, 512)),
                            reads=pnames(i, 512), writes=["A.vst%d" % vr])
                        dma("act", kvv[o5 * 4:(o5 + 1) * 4, c * 4 + i, :, :]
                            .rearrange("h p d -> p h d"),
                            A_VST(vr).rearrange("p (h d) -> p h d", h=4),
                            reads=["A.vst%d" % vr], writes=["kvv"])
            if is_k:
                for fc in range(FC):
                    kr = fc % 2
                    for i in range(4):
                        pe(lambda e, fc=fc, i=i: e.transpose(
                            psbf(4, 512)[:, i * 128:(i + 1) * 128],
                            A_KT(i)[:, fc * 128:(fc + 1) * 128], IDENT_BF),
                           reads=a_ktnames(i) + ["identb"], writes=["ps4"])
                    act(lambda e, kr=kr: e.copy(A_KFM(kr), psbf(4, 512)),
                        reads=["ps4"], writes=["A.kfm%d" % kr])
                    dma("act", kvk3[:, fc, c * 512:(c + 1) * 512], A_KFM(kr),
                        reads=["A.kfm%d" % kr], writes=["kvk"])
    finalize_bound("A", NKA, MAXK2A, ovf(49152, 128), "maxk2a")
    P.fence()

    dma("sync", xres[:].rearrange("p (fc t) -> p fc t", fc=FC), x_proc3,
        writes=["x%d" % fc for fc in range(FC)])
    ln_main("L1.", TCS, False)
    P.fence()

    C_QH = lambda r: ovb(r * 3080, 1540)
    C_PT = lambda r: ovb(6160 + r * 1024, 512)
    C_PTT = ovb(9232, 256)
    C_KBH = lambda r: ovb(9744 + r * 8192, 4096)
    C_VBH = lambda r: ovb(26128 + r * 8192, 4096)
    C_WQS = ovb(42512, 2048)
    C_QTM = lambda r: ovb(46608 + r * 256, 128)
    C_QRAW = lambda r: ovb(47120 + r * 256, 128)
    C_TC = lambda r: ovb(47632 + r * 512, 256)
    C_TS = lambda r: ovb(48656 + r * 512, 256)
    C_AOS = lambda r: ovb(49680 + r * 1024, 512)
    C_O1N = ovf(51728, 128)
    C_O2N = ovf(52240, 128)
    C_SQS = ovf(52752, 128)
    C_NSC = ovb(53264, 128)
    C_RS1 = ovb(53520, 64)
    C_RS2 = ovb(53648, 64)
    C_BROW = ovf(53776, 128)

    BLKC3 = [(0, 512), (512, 512), (1024, 512)]

    def c_prod_steps(h):
        hr = h % 2
        steps = []
        steps.append(lambda h=h: dma(
            "pool", C_WQS.rearrange("p (ec o) -> p ec o", ec=FC),
            wqa[:, :, h * 128:(h + 1) * 128].rearrange("ec p o -> p ec o"),
            writes=["C.wqs"]))
        steps.append(lambda h=h, hr=hr: dma(
            "sync", C_KBH(hr), kvk[h * 128:(h + 1) * 128, :],
            reads=["kvk"], writes=["C.kbh%d" % hr]))
        steps.append(lambda h=h, hr=hr: dma(
            "sync", C_VBH(hr).rearrange("p (c d) -> p c d", c=32),
            kvv[h, :, :, :].rearrange("c p d -> p c d"),
            reads=["kvv"], writes=["C.vbh%d" % hr]))

        def qstep(i, off, n, hr=hr):
            ir = i % 2
            mmg([(XH(ec, off, n), C_WQS[:, ec * 128:(ec + 1) * 128],
                  psv(0, 128)[0:n, :]) for ec in range(FC)],
                reads=["C.wqs"] + [XHN(ec) for ec in range(FC)],
                writes=pnames(0, 128))
            act(lambda e, n=n, ir=ir: e.copy(C_QRAW(ir)[0:n, :], psv(0, 128)[0:n, :]),
                reads=pnames(0, 128), writes=["C.qraw%d" % ir])
            norm_acc("C", C_QRAW(ir)[0:n, :], C_NSC[0:n, :], NQA, first=(i == 0),
                     rd=["C.qraw%d" % ir], n=n)
            dma("sync", C_TC(ir)[0:n, :], ca_pr[off:off + n, :], writes=["C.tc%d" % ir])
            dma("sync", C_TS(ir)[0:n, :], sa_pr[off:off + n, :], writes=["C.ts%d" % ir])
            v4 = C_QRAW(ir)[0:n, :].rearrange("p (rh f two) -> p rh f two", rh=2, two=2)
            qo = C_QTM(ir)[0:n, :].rearrange("p (rh f two) -> p rh f two", rh=2, two=2)
            c3 = C_TC(ir)[0:n, 0:64].rearrange("p (rh f) -> p rh f", rh=2)
            s3 = C_TS(ir)[0:n, 0:64].rearrange("p (rh f) -> p rh f", rh=2)
            t1 = C_RS1[0:n, :].rearrange("p (rh f) -> p rh f", rh=2)
            t2 = C_RS2[0:n, :].rearrange("p (rh f) -> p rh f", rh=2)
            rope6("C.r", v4[:, :, :, 0], v4[:, :, :, 1], c3, s3,
                  qo[:, :, :, 0], qo[:, :, :, 1],
                  rd=["C.qraw%d" % ir, "C.tc%d" % ir, "C.ts%d" % ir],
                  wr=["C.qtm%d" % ir], t1=t1, t2=t2)
            pe(lambda e, n=n, ir=ir: e.transpose(psbf(7, 256)[:, 0:n],
                                                 C_QTM(ir)[0:n, :],
                                                 IDENT_BF[0:n, 0:n]),
               reads=["C.qtm%d" % ir, "identb"], writes=["ps7"])
            act(lambda e, off=off, n=n: e.copy(C_QH(hr)[:, off:off + n],
                                               psbf(7, 256)[:, 0:n]),
                reads=["ps7"], writes=["C.qh%d" % hr])
        for i, (off, n) in enumerate(T128):
            steps.append(lambda i=i, off=off, n=n: qstep(i, off, n))

        def bound(h=h):
            pe(lambda e: e.transpose(psv(7, 128)[0:1, :], NQA, IDENT_F),
               reads=["C.ncol", "identf"], writes=["ps7"])
            act(lambda e: e.copy(C_BROW[0:1, 0:128], psv(7, 128)[0:1, :]),
                reads=["ps7"], writes=["C.nrow"])
            dve(lambda e: e.reduce_max(T0, C_BROW[0:1, 0:128], axis=AX.X),
                reads=["C.nrow"], writes=["C.b"])
            dve(lambda e: e.tensor_tensor(T0, T0, MAXK2A, op=ALU.mult),
                reads=["C.b", "maxk2a"], writes=["C.b"])
            act(lambda e: e.activation(T0, T0, AF.Ln), reads=["C.b"], writes=["C.b"])
            act(lambda e: e.activation(T0, T0, AF.Exp, scale=0.5),
                reads=["C.b"], writes=["C.b"])
            act(lambda e: e.activation(T0, T0, AF.Copy, scale=-SCALE_A),
                reads=["C.b"], writes=["C.b"])
            mmg([(ONESR, T0, psv(7, 1))], reads=["onesr", "C.b"], writes=["ps7"])
            act(lambda e, h=h: e.copy(BA(h), psv(7, 1)), reads=["ps7"],
                writes=["Ba%d" % h])
        steps.append(bound)
        return steps

    def c_epilogue(h, qn, tail):
        hr = h % 2
        for qi, (o2, n2) in enumerate(fchunks(qn, 128)):
            if tail:
                o1 = psv(4, 512)[0:n2, 0:128]
                o2t = psv(5, 512)[0:n2, 0:128]
                d1 = psv(6, 16)[0:n2, 8:9]
                d2 = psv(6, 16)[0:n2, 9:10]
            else:
                o1 = psv(4, 512)[0:n2, qi * 128:(qi + 1) * 128]
                o2t = psv(5, 512)[0:n2, qi * 128:(qi + 1) * 128]
                d1 = psv(6, 16)[0:n2, qi:qi + 1]
                d2 = psv(6, 16)[0:n2, 4 + qi:5 + qi]
            dve(lambda e, d1=d1, n2=n2: e.reciprocal(EP1[0:n2, :], d1),
                reads=["ps6"], writes=["C.e1"])
            dve(lambda e, d2=d2, n2=n2: e.reciprocal(EP2[0:n2, :], d2),
                reads=["ps6"], writes=["C.e2"])
            dve(lambda e, o1=o1, n2=n2: e.tensor_scalar(
                C_O1N[0:n2, :], o1, EP1[0:n2, :], None, op0=ALU.mult),
                reads=["ps4", "C.e1"], writes=["C.o1n"])
            dve(lambda e, o2t=o2t, n2=n2: e.tensor_scalar(
                C_O2N[0:n2, :], o2t, EP2[0:n2, :], None, op0=ALU.mult),
                reads=["ps5", "C.e2"], writes=["C.o2n"])
            dve(lambda e, n2=n2: e.scalar_tensor_tensor(
                C_O2N[0:n2, :], C_O2N[0:n2, :], LAMC[0:n2, :], C_O1N[0:n2, :],
                op0=ALU.mult, op1=ALU.subtract),
                reads=["C.o1n", "C.o2n", "lam"], writes=["C.o2n"])
            act(lambda e, n2=n2: e.activation(C_SQS[0:n2, :], C_O2N[0:n2, :], AF.Square,
                                              accum_out=EP3[0:n2, :]),
                reads=["C.o2n"], writes=["C.ep3", "C.sqs"])
            act(lambda e, n2=n2: e.activation(EP3[0:n2, :], EP3[0:n2, :], AF.Ln,
                                              bias=EPSC[0:n2, :], scale=1.0 / 128),
                reads=["C.ep3", "epsc"], writes=["C.ep3"])
            act(lambda e, n2=n2: e.activation(EP3[0:n2, :], EP3[0:n2, :], AF.Exp,
                                              scale=-0.5),
                reads=["C.ep3"], writes=["C.ep3"])
            dve(lambda e, n2=n2: e.tensor_scalar(
                C_O2N[0:n2, :], C_O2N[0:n2, :], EP3[0:n2, :], None, op0=ALU.mult),
                reads=["C.ep3", "C.o2n"], writes=["C.o2n"])
            ar = qi % 2
            dve(lambda e, n2=n2, ar=ar: e.tensor_tensor(
                C_AOS(ar)[0:n2, 0:128], C_O2N[0:n2, :], GAMN[0:n2, :], op=ALU.mult),
                reads=["C.o2n", "gamn"], writes=["C.ao%d" % ar])
            pe(lambda e, ar=ar, n2=n2: e.transpose(psbf(7, 256)[:, 0:n2],
                                                   C_AOS(ar)[0:n2, 0:128],
                                                   IDENT_BF[0:n2, 0:n2]),
               reads=["C.ao%d" % ar, "identb"], writes=["ps7"])
            act(lambda e, ar=ar, n2=n2: e.copy(C_AOS(ar)[:, 256:256 + n2],
                                               psbf(7, 256)[:, 0:n2]),
                reads=["ps7"], writes=["C.aot%d" % ar])
            qdst = 1536 if tail else c_qoff[0] + o2
            dma("act", afm[h * 128:(h + 1) * 128, qdst:qdst + n2],
                C_AOS(ar)[:, 256:256 + n2],
                reads=["C.aot%d" % ar], writes=["afm"])

    c_qoff = [0]

    def c_tail(h):
        hr = h % 2
        pairs = [(C_KBH(hr)[s * 64:(s + 1) * 64, tkc * 128:(tkc + 1) * 128],
                  C_QH(hr)[s * 64:(s + 1) * 64, 1536:1540],
                  psv(1, 256)[:, (s * 32 + tkc) * 4:(s * 32 + tkc) * 4 + 4],
                  s == 0 and tkc == 0, s == 1 and tkc == 31)
                 for s in (0, 1) for tkc in range(32)]
        mmgf(pairs, reads=["C.kbh%d" % hr, "C.qh%d" % hr], writes=["ps1"])
        act(lambda e, h=h: e.activation(C_PTT[:, 0:256], psv(1, 256), AF.Exp,
                                        bias=BA(h), scale=SCALE_A),
            reads=["ps1", "Ba%d" % h], writes=["C.ptt"])
        pairs = []
        for s in (0, 1):
            for tkc in range(32):
                off = (s * 32 + tkc) * 4
                pairs.append((C_PTT[:, off:off + 4],
                              C_VBH(hr)[:, tkc * 128:(tkc + 1) * 128],
                              psv(4 + s, 512)[0:4, 0:128], tkc == 0, tkc == 31))
                pairs.append((C_PTT[:, off:off + 4], ONESC_BF,
                              psv(6, 16)[0:4, 8 + s:9 + s],
                              s == 0 and tkc == 0, s == 1 and tkc == 31))
        mmgf(pairs, reads=["C.ptt", "C.vbh%d" % hr, "onescb"],
             writes=["ps4", "ps5", "ps6"])
        c_epilogue(h, 4, True)

    def c_emit_head(h, prod_steps):
        hr = h % 2
        scores, exps, pvs = [], [], []
        qb_end = {}
        it = 0
        for (qoff, qn) in BLKC3:
            for s in (0, 1):
                for tkc in range(32):
                    bank = 1 + it % 3
                    slot = it % 3

                    def sc(hr=hr, s=s, tkc=tkc, qoff=qoff, qn=qn, bank=bank):
                        mmg([(C_KBH(hr)[s * 64:(s + 1) * 64,
                                        tkc * 128:(tkc + 1) * 128],
                              C_QH(hr)[s * 64:(s + 1) * 64, qoff:qoff + qn],
                              psv(bank, qn))],
                            reads=["C.kbh%d" % hr, "C.qh%d" % hr],
                            writes=["ps%d" % bank])

                    def ex(h=h, bank=bank, slot=slot, qn=qn):
                        act(lambda e: e.activation(C_PT(slot)[:, 0:qn],
                                                   psv(bank, qn), AF.Exp,
                                                   bias=BA(h), scale=SCALE_A),
                            reads=["ps%d" % bank, "Ba%d" % h],
                            writes=["C.pt%d" % slot])

                    def pv(hr=hr, s=s, tkc=tkc, slot=slot):
                        pairs = []
                        for qi in range(4):
                            pairs.append((C_PT(slot)[:, qi * 128:(qi + 1) * 128],
                                          C_VBH(hr)[:, tkc * 128:(tkc + 1) * 128],
                                          psv(4 + s, 512)[:, qi * 128:(qi + 1) * 128],
                                          tkc == 0 and qi == 0,
                                          tkc == 31 and qi == 3))
                            pairs.append((C_PT(slot)[:, qi * 128:(qi + 1) * 128],
                                          ONESC_BF,
                                          psv(6, 16)[:, s * 4 + qi:s * 4 + qi + 1],
                                          s == 0 and tkc == 0 and qi == 0,
                                          s == 1 and tkc == 31 and qi == 3))
                        mmgf(pairs,
                             reads=["C.pt%d" % slot, "C.vbh%d" % hr, "onescb"],
                             writes=["ps%d" % (4 + s), "ps6"])
                    scores.append(sc); exps.append(ex); pvs.append(pv)
                    it += 1
            qb_end[it - 1] = (qoff, qn)
        N = len(scores)
        pi = [0]
        stride = max(2, (N + 4) // (len(prod_steps) + 1)) if prod_steps else N + 9

        def flush_pv(j):
            pvs[j]()
            if j in qb_end:
                qoff, qn = qb_end[j]
                c_qoff[0] = qoff
                c_epilogue(h, qn, False)
        for t in range(N):
            scores[t]()
            if t >= 1:
                exps[t - 1]()
            if t >= 2:
                flush_pv(t - 2)
            if prod_steps and t % stride == stride - 1 and pi[0] < len(prod_steps):
                prod_steps[pi[0]]()
                pi[0] += 1
        exps[N - 1]()
        flush_pv(N - 2)
        flush_pv(N - 1)
        while pi[0] < len(prod_steps):
            prod_steps[pi[0]]()
            pi[0] += 1
        c_tail(h)

    for st in c_prod_steps(0):
        st()
    for h in range(H):
        c_emit_head(h, c_prod_steps(h + 1) if h + 1 < H else [])
    P.fence()

    D_WS = lambda r: ovb(r * 4096, 2048)
    for fc in range(FC):
        dma("sync", XH(fc), afm3[:, fc, :], reads=["afm"], writes=[XHN(fc)])
    for oc in range(16):
        orr = oc % 2
        dma("pool", D_WS(orr).rearrange("p (fc o) -> p fc o", fc=FC), woa[oc],
            writes=["D.ws%d" % orr])
        for bi, (off, n) in enumerate(TCS):
            pk = bi % 2
            mmg([(D_WS(orr)[:, fc * 128:(fc + 1) * 128], XH(fc, off, n),
                  psv(2 * pk, n)) for fc in range(FC)],
                reads=["D.ws%d" % orr] + [XHN(fc) for fc in range(FC)],
                writes=pnames(2 * pk, n))
            dve(lambda e, oc=oc, off=off, n=n, pk=pk: e.tensor_tensor(
                XC(oc, off, n), XC(oc, off, n), psv(2 * pk, n), op=ALU.add),
                reads=pnames(2 * pk, n) + [XN(oc)], writes=[XN(oc)])
    if "x:attn_a" in taps:
        for fc in range(FC):
            dma("sync", tap_outs["x:attn_a"].rearrange("(fc p) t -> p fc t", p=128)[:, fc, :],
                XC(fc), reads=[XN(fc)], writes=["tapxa"])
    P.fence()

    FA = lambda mc: ovb(mc * 3080, 1540)
    F_R = ovf(0, TP)[0:1, :]
    F_MR = ovf(6160, TP)[0:1, :]
    F_LGT = ovf(12320, TP)[0:16, :]
    F_SC = ovf(18480, 16)
    F_SCW = ovf(18544, 16)
    F_MX = ovf(18608, 1)
    F_GE = ovf(18612, 16)
    F_CWTM = lambda i: ovf(18676 + i * 64, 16)
    F_GSL = lambda r: ovb(33880 + r * 1024, 512)
    F_USL = lambda r: ovb(35928 + r * 1024, 512)
    F_DSL = lambda r: ovb(37976 if r == 0 else LN_SCR + 3080, 1408)
    F_CWTF = ovb(40792, 1540)
    F_CWB = ovb(43872, 1540)
    F_SG = ovb(LN_SCR + 6144, 512)
    F_ROWT = ovb(LN_SCR, 1540)

    def gate_and_moe(tag, ncols, tcsl, t128l):
        CWT = F_CWTF[0:16, 0:ncols]
        for (off, n) in tcsl:
            mmg([(gts[:, ec * NEXP:(ec + 1) * NEXP], XC(ec, off, n),
                  psv(4, n)[0:NEXP, :]) for ec in range(FC)],
                reads=["gts"] + [XN(ec) for ec in range(FC)], writes=pnames(4, n))
            dve(lambda e, off=off, n=n: e.tensor_copy(F_LGT[:, off:off + n],
                                                      psv(4, n)[0:NEXP, :]),
                reads=pnames(4, n), writes=[tag + "lg"])
        for (off, n) in tcsl:
            mmg([(ONESR[:, 0:16], F_R[0:1, off:off + n], psv(4, n)[0:NEXP, :])],
                reads=["onesr", tag + "R"], writes=pnames(4, n))
            dve(lambda e, off=off, n=n: e.tensor_tensor(
                F_LGT[:, off:off + n], F_LGT[:, off:off + n], psv(4, n)[0:NEXP, :],
                op=ALU.mult),
                reads=pnames(4, n) + [tag + "lg"], writes=[tag + "lg"])
            mmg([(ONESR[:, 0:16], F_MR[0:1, off:off + n], psv(4, n)[0:NEXP, :])],
                reads=["onesr", tag + "MR"], writes=pnames(4, n))
            dve(lambda e, off=off, n=n: e.tensor_scalar(
                psv(6, n)[0:NEXP, :], psv(4, n)[0:NEXP, :], COLG, None, op0=ALU.mult),
                reads=pnames(4, n) + ["colg"], writes=pnames(6, n))
            dve(lambda e, off=off, n=n: e.tensor_tensor(
                F_LGT[:, off:off + n], F_LGT[:, off:off + n], psv(6, n)[0:NEXP, :],
                op=ALU.subtract),
                reads=pnames(6, n) + [tag + "lg"], writes=[tag + "lg"])
        for i, (off, n) in enumerate(t128l):
            pe(lambda e, off=off, n=n: e.transpose(psv(4, 16)[0:n, :],
                                                   F_LGT[:, off:off + n],
                                                   IDENT_F[0:16, 0:16]),
               reads=[tag + "lg", "identf"], writes=pnames(4, 16))
            act(lambda e, n=n: e.activation(F_SC[0:n, :], psv(4, 16)[0:n, :], AF.Exp),
                reads=pnames(4, 16), writes=[tag + "e"])
            dve(lambda e, n=n: e.reduce_sum(F_MX[0:n, :], F_SC[0:n, :], axis=AX.X),
                reads=[tag + "e"], writes=[tag + "s"])
            dve(lambda e, n=n: e.reciprocal(F_MX[0:n, :], F_MX[0:n, :]),
                reads=[tag + "s"], writes=[tag + "s"])
            dve(lambda e, n=n: e.tensor_scalar(F_SC[0:n, :], F_SC[0:n, :], F_MX[0:n, :],
                                               None, op0=ALU.mult),
                reads=[tag + "s", tag + "e"], writes=[tag + "e"])
            dve(lambda e, n=n: e.tensor_copy(F_SCW[0:n, :], F_SC[0:n, :]),
                reads=[tag + "e"], writes=[tag + "w"])
            for it in range(TOPK):
                dve(lambda e, n=n: e.reduce_max(F_MX[0:n, :], F_SCW[0:n, :], axis=AX.X),
                    reads=[tag + "w"], writes=[tag + "mx"])
                if it < TOPK - 1:
                    dve(lambda e, n=n: e.tensor_scalar(F_GE[0:n, :], F_SCW[0:n, :],
                                                       F_MX[0:n, :], None, op0=ALU.is_ge),
                        reads=[tag + "w", tag + "mx"], writes=[tag + "g"])
                    dve(lambda e, n=n: e.scalar_tensor_tensor(
                        F_SCW[0:n, :], F_GE[0:n, :], NEGBIG[0:n, :], F_SCW[0:n, :],
                        op0=ALU.mult, op1=ALU.add),
                        reads=[tag + "g", tag + "w", "negbig"], writes=[tag + "w"])
            dve(lambda e, n=n: e.tensor_scalar(F_GE[0:n, :], F_SC[0:n, :], F_MX[0:n, :],
                                               None, op0=ALU.is_ge),
                reads=[tag + "e", tag + "mx"], writes=[tag + "g"])
            dve(lambda e, i=i, n=n: e.tensor_tensor(F_CWTM(i)[0:n, :], F_SC[0:n, :],
                                                    F_GE[0:n, :], op=ALU.mult),
                reads=[tag + "g", tag + "e"], writes=[tag + "cw%d" % i])
        for i, (off, n) in enumerate(t128l):
            pe(lambda e, i=i, n=n: e.transpose(psv(4, 128)[0:16, 0:n], F_CWTM(i)[0:n, :],
                                               IDENT_F[0:n, 0:n]),
               reads=[tag + "cw%d" % i, "identf"], writes=pnames(4, 128))
            act(lambda e, off=off, n=n: e.copy(CWT[:, off:off + n], psv(4, 128)[0:16, 0:n]),
                reads=pnames(4, 128), writes=[tag + "cwT"])
        P.fence()
        for ex in range(NEXP):
            mmg([(IDENT_BF[0:16, ex:ex + 1], CWT[:, o2:o2 + n2],
                  psv(4, ncols)[0:1, o2:o2 + n2]) for (o2, n2) in tcsl],
                reads=[tag + "cwT", "identb"], writes=pnames(4, ncols),
                each_start=True)
            act(lambda e: e.copy(F_ROWT[0:1, 0:ncols], psv(4, ncols)[0:1, :]),
                reads=pnames(4, ncols), writes=[tag + "rowT"])
            mmg([(ONESR_BF, F_ROWT[0:1, o2:o2 + n2], psv(4, ncols)[:, o2:o2 + n2])
                 for (o2, n2) in tcsl],
                reads=[tag + "rowT", "onesrb"], writes=pnames(4, ncols),
                each_start=True)
            dve(lambda e: e.tensor_copy(F_CWB[:, 0:ncols], psv(4, ncols)),
                reads=pnames(4, ncols), writes=[tag + "cwb"])
            for mc in range(MCN):
                for qh in range(4):
                    r = qh % 2
                    dma("pool", F_GSL(r).rearrange("p (ec m) -> p ec m", ec=4),
                        wg[ex, mc, :, qh * 4:(qh + 1) * 4, :].rearrange("p ec m -> p ec m"),
                        writes=[tag + "gs%d" % r])
                    dma("pool", F_USL(r).rearrange("p (ec m) -> p ec m", ec=4),
                        wu[ex, mc, :, qh * 4:(qh + 1) * 4, :].rearrange("p ec m -> p ec m"),
                        writes=[tag + "us%d" % r])
                    for ti, (off, n) in enumerate(tcsl):
                        mmg([(F_GSL(r)[:, k2 * 128:(k2 + 1) * 128],
                              XH(qh * 4 + k2, off, n), psv(ti, n)) for k2 in range(4)],
                            reads=[tag + "gs%d" % r] + [XHN(qh * 4 + k2) for k2 in range(4)],
                            writes=pnames(ti, n), start=(qh == 0), stop=(qh == 3))
                        mmg([(F_USL(r)[:, k2 * 128:(k2 + 1) * 128],
                              XH(qh * 4 + k2, off, n), psv(4 + ti, n)) for k2 in range(4)],
                            reads=[tag + "us%d" % r] + [XHN(qh * 4 + k2) for k2 in range(4)],
                            writes=pnames(4 + ti, n), start=(qh == 0), stop=(qh == 3))
                for ti, (off, n) in enumerate(tcsl):
                    act(lambda e, ti=ti, n=n: e.activation(F_SG[:, 0:n], psv(ti, n), AF.Silu),
                        reads=pnames(ti, n), writes=[tag + "sg"])
                    dve(lambda e, ti=ti, n=n: e.tensor_tensor(F_SG[:, 0:n], F_SG[:, 0:n],
                                                              psv(4 + ti, n), op=ALU.mult),
                        reads=pnames(4 + ti, n) + [tag + "sg"], writes=[tag + "sg"])
                    dve(lambda e, mc=mc, off=off, n=n: e.tensor_tensor(
                        FA(mc)[:, off:off + n], F_SG[:, 0:n], F_CWB[:, off:off + n],
                        op=ALU.mult),
                        reads=[tag + "sg", tag + "cwb"], writes=[tag + "a%d" % mc])
            for oc in range(16):
                dr = oc % 2
                dma("pool", F_DSL(dr).rearrange("p (mc o) -> p mc o", mc=MCN),
                    wd[ex, oc], writes=[tag + "ds%d" % dr])
                for ti, (off, n) in enumerate(tcsl):
                    pg = 4 + ti
                    mmg([(F_DSL(dr)[:, mc * 128:(mc + 1) * 128], FA(mc)[:, off:off + n],
                          psv(pg, n)) for mc in range(MCN)],
                        reads=[tag + "ds%d" % dr] + [tag + "a%d" % mc for mc in range(MCN)],
                        writes=pnames(pg, n))
                    dve(lambda e, oc=oc, off=off, n=n, pg=pg: e.tensor_tensor(
                        XC(oc, off, n), XC(oc, off, n), psv(pg, n), op=ALU.add),
                        reads=pnames(pg, n) + [XN(oc)], writes=[XN(oc)])

    ln_main("F1.", TCS, True)
    gate_and_moe("F1.", TP, TCS, T128)
    if "x:ffn1" in taps:
        for fc in range(FC):
            dma("sync", tap_outs["x:ffn1"].rearrange("(fc p) t -> p fc t", p=128)[:, fc, :],
                XC(fc), reads=[XN(fc)], writes=["tapf1"])
    P.fence()

    ln_main("L3.", TCS, False)
    P.fence()

    G_KB = lambda g: ovb(g * 3080, 1540)
    G_VB = ovb(12320, 6708)
    G_PSL = lambda r: ovb(25736 + r * 4096, 2048)
    G_KRAW = ovb(33928, 512)
    G_KTM = ovb(34952, 512)
    G_TC = lambda r: ovb(35976 + r * 512, 256)
    G_TS = lambda r: ovb(37000 + r * 512, 256)
    G_BK = ovb(38024, 512)
    G_BV = ovb(39048, 512)
    G_PR1 = ovb(40072, 256)
    G_PR2 = ovb(40584, 256)
    G_NSC = ovb(41096, 512)

    pool(lambda e: e.memset(G_VB, 1.0), writes=["G.vb"])
    dma("sync", ovf(LN_SCR, 512)[0:1, :], bkb, writes=["G.bkr"])
    mmg([(ONESR, ovf(LN_SCR, 512)[0:1, :], psv(0, 512))], reads=["onesr", "G.bkr"],
        writes=pnames(0, 512))
    act(lambda e: e.copy(G_BK, psv(0, 512)), reads=pnames(0, 512), writes=["G.bk"])
    dma("sync", ovf(LN_SCR, 512)[0:1, :], bvb, writes=["G.bkr"])
    mmg([(ONESR, ovf(LN_SCR, 512)[0:1, :], psv(0, 512))], reads=["onesr", "G.bkr"],
        writes=pnames(0, 512))
    act(lambda e: e.copy(G_BV, psv(0, 512)), reads=pnames(0, 512), writes=["G.bv"])

    IGRP = [list(range(0, 6)), list(range(6, 12)), [12]]
    for mat, wsrc, is_k in (("k", wkb, True), ("v", wvb, False)):
        for grp in IGRP:
            for qh in range(4):
                r = qh % 2
                dma("pool", G_PSL(r).rearrange("p (ec o) -> p ec o", ec=4),
                    wsrc[qh * 4:(qh + 1) * 4, :, :].rearrange("ec p o -> p ec o"),
                    writes=["G.psl%d" % r])
                for gi, i in enumerate(grp):
                    off, n = T128[i]
                    mmg([(XH(qh * 4 + k2, off, n), G_PSL(r)[:, k2 * 512:(k2 + 1) * 512],
                          psv(gi, 512)[0:n, :]) for k2 in range(4)],
                        reads=["G.psl%d" % r] + [XHN(qh * 4 + k2) for k2 in range(4)],
                        writes=pnames(gi, 512), start=(qh == 0), stop=(qh == 3))
            for gi, i in enumerate(grp):
                off, n = T128[i]
                if is_k:
                    dve(lambda e, gi=gi, n=n: e.tensor_tensor(
                        G_KRAW[0:n, :], psv(gi, 512)[0:n, :], G_BK[0:n, :], op=ALU.add),
                        reads=pnames(gi, 512) + ["G.bk"], writes=["G.kraw"])
                    norm_acc("G", G_KRAW[0:n, :], G_NSC[0:n, :], NKB, first=(i == 0),
                             rd=["G.kraw"], n=n)
                    ir = i % 2
                    dma("act", G_TC(ir)[0:n, :], cb_pr[off:off + n, :],
                        writes=["G.tc%d" % ir])
                    dma("act", G_TS(ir)[0:n, :], sb_pr[off:off + n, :],
                        writes=["G.ts%d" % ir])
                    v3 = G_KRAW[0:n, :].rearrange("p (g half f) -> p g half f", g=4, half=2)
                    ko = G_KTM[0:n, :].rearrange("p (g half f) -> p g half f", g=4, half=2)
                    c3 = G_TC(ir)[0:n, :].rearrange("p (g f) -> p g f", g=4)
                    s3 = G_TS(ir)[0:n, :].rearrange("p (g f) -> p g f", g=4)
                    t1 = G_PR1[0:n, :].rearrange("p (g f) -> p g f", g=4)
                    t2 = G_PR2[0:n, :].rearrange("p (g f) -> p g f", g=4)
                    rope6("G.r", v3[:, :, 0, :], v3[:, :, 1, :], c3, s3,
                          ko[:, :, 0, :], ko[:, :, 1, :],
                          rd=["G.kraw", "G.tc%d" % ir, "G.ts%d" % ir], wr=["G.ktm"],
                          t1=t1, t2=t2)
                    for g in range(4):
                        pe(lambda e, g=g, n=n: e.transpose(
                            psbf(6, 1024)[:, g * 256:g * 256 + n],
                            G_KTM[0:n, g * 128:(g + 1) * 128], IDENT_BF[0:n, 0:n]),
                           reads=["G.ktm", "identb"], writes=["ps6"])
                        act(lambda e, g=g, off=off, n=n: e.copy(
                            G_KB(g)[:, off:off + n],
                            psbf(6, 1024)[:, g * 256:g * 256 + n]),
                            reads=["ps6"], writes=["G.kb%d" % g])
                else:
                    dve(lambda e, gi=gi, n=n, i=i: e.tensor_tensor(
                        G_VB[0:n, i * 516:(i + 1) * 516]
                        .rearrange("p (g d) -> p g d", d=129)[:, :, 0:128],
                        psv(gi, 512)[0:n, :].rearrange("p (g d) -> p g d", d=128),
                        G_BV[0:n, :].rearrange("p (g d) -> p g d", d=128), op=ALU.add),
                        reads=pnames(gi, 512) + ["G.bv", "G.vb"], writes=["G.vb"])
    finalize_bound("G", NKB, MAXK2B, ovf(LN_SCR, 128), "maxk2b")
    P.fence()

    G_QSL = lambda r: ovb(25736 + r * 4096, 2048)
    G_MSK = lambda r: ovb(33928 + r * 1024, 512)
    G_PT = lambda r: ovb(35976 + r * 1024, 512)
    G_QRAW = lambda r: ovb(39048 + r * 256, 128)
    G_TC2 = lambda r: ovb(39560 + r * 512, 256)
    G_TS2 = lambda r: ovb(40584 + r * 512, 256)
    G_BQ = ovb(41608, 128)
    G_AR1 = lambda r: ovb(41864 + r * 256, 128)
    G_AR2 = lambda r: ovb(42376 + r * 256, 128)
    G_QH = lambda r: ovb(42888 + r * 2048, 1024)
    G_AO = lambda r: ovb(46984 + r * 512, 256)
    G_NSQ = ovb(48008, 128)
    G_BROW = ovf(48264, 128)
    G_BQROW = ovf(48776, 128)

    def c_otile(s, qi):
        base = 4 + 2 * s
        woff = qi * 129 if qi < 3 else 512 + (qi - 3) * 129
        return psv(base, 1024)[:, woff:woff + 129]

    BLKB = [(0, 512), (512, 512)]

    def g_prod_steps(h):
        hr = h % 2
        steps = []

        def bqstep(h=h):
            dma("sync", G_BQROW[0:1, :], bqb[0:1, h * 128:(h + 1) * 128],
                writes=["G.bqr"])
            mmg([(ONESR, G_BQROW[0:1, :], psv(7, 128))],
                reads=["onesr", "G.bqr"], writes=["ps7"])
            act(lambda e: e.copy(G_BQ, psv(7, 128)), reads=["ps7"], writes=["G.bq"])
        steps.append(bqstep)
        steps.append(lambda h=h, hr=hr: dma(
            "pool", G_QSL(hr).rearrange("p (ec o) -> p ec o", ec=FC),
            wqb[:, :, h * 128:(h + 1) * 128].rearrange("ec p o -> p ec o"),
            writes=["G.qsl%d" % hr]))

        def qstep(i, off, n, hr=hr):
            ir = i % 2
            mmg([(XH(ec, off, n), G_QSL(hr)[:, ec * 128:(ec + 1) * 128],
                  psv(0, 128)[0:n, :]) for ec in range(FC)],
                reads=["G.qsl%d" % hr] + [XHN(ec) for ec in range(FC)],
                writes=pnames(0, 128))
            dve(lambda e, n=n, ir=ir: e.tensor_tensor(
                G_QRAW(ir)[0:n, :], psv(0, 128)[0:n, :], G_BQ[0:n, :], op=ALU.add),
                reads=pnames(0, 128) + ["G.bq"], writes=["G.qraw%d" % ir])
            norm_acc("Gq", G_QRAW(ir)[0:n, :], G_NSQ[0:n, :], NQA, first=(i == 0),
                     rd=["G.qraw%d" % ir], n=n)
            dma("sync", G_TC2(ir)[0:n, :], cb_pr[off:off + n, :],
                writes=["G.tc2%d" % ir])
            dma("sync", G_TS2(ir)[0:n, :], sb_pr[off:off + n, :],
                writes=["G.ts2%d" % ir])
            v3 = G_QRAW(ir)[0:n, :].rearrange("p (half f) -> p half f", half=2)
            ko = G_AR1(ir)[0:n, :].rearrange("p (half f) -> p half f", half=2)
            c3 = G_TC2(ir)[0:n, 0:64]
            s3 = G_TS2(ir)[0:n, 0:64]
            rope6("Gq.r", v3[:, 0, :], v3[:, 1, :], c3, s3,
                  ko[:, 0, :], ko[:, 1, :],
                  rd=["G.qraw%d" % ir, "G.tc2%d" % ir, "G.ts2%d" % ir],
                  wr=["G.qk%d" % ir],
                  t1=G_AR2(ir)[0:n, 0:64], t2=G_AR2(ir)[0:n, 64:128])
            pe(lambda e, n=n, ir=ir: e.transpose(psbf(6, 256)[:, 0:n],
                                                 G_AR1(ir)[0:n, :],
                                                 IDENT_BF[0:n, 0:n]),
               reads=["G.qk%d" % ir, "identb"], writes=["ps6"])
            act(lambda e, off=off, n=n: e.copy(G_QH(hr)[:, off:off + n],
                                               psbf(6, 256)[:, 0:n]),
                reads=["ps6"], writes=["G.qh%d" % hr])
        for i, (off, n) in enumerate(T128_OWN):
            steps.append(lambda i=i, off=off, n=n: qstep(i, off, n))

        def bound(h=h):
            pe(lambda e: e.transpose(psv(7, 128)[0:1, :], NQA, IDENT_F),
               reads=["Gq.ncol", "identf"], writes=["ps7"])
            act(lambda e: e.copy(G_BROW[0:1, 0:128], psv(7, 128)[0:1, :]),
                reads=["ps7"], writes=["Gq.nrow"])
            dve(lambda e: e.reduce_max(T0, G_BROW[0:1, 0:128], axis=AX.X),
                reads=["Gq.nrow"], writes=["G.b"])
            dve(lambda e: e.tensor_tensor(T0, T0, MAXK2B, op=ALU.mult),
                reads=["G.b", "maxk2b"], writes=["G.b"])
            act(lambda e: e.activation(T0, T0, AF.Ln), reads=["G.b"], writes=["G.b"])
            act(lambda e: e.activation(T0, T0, AF.Exp, scale=0.5),
                reads=["G.b"], writes=["G.b"])
            act(lambda e: e.activation(T0, T0, AF.Copy, scale=-SCALE_B),
                reads=["G.b"], writes=["G.b"])
            mmg([(ONESR, T0, psv(7, 1))], reads=["onesr", "G.b"], writes=["ps7"])
            act(lambda e, h=h: e.copy(BB(h), psv(7, 1)), reads=["ps7"],
                writes=["Bb%d" % h])
        steps.append(bound)
        return steps

    def g_epilogue(h, qoff):
        for qi in range(4):
            o1 = c_otile(0, qi)
            dve(lambda e, o1=o1: e.reciprocal(EP1, o1[:, 128:129]),
                reads=["ps4", "ps5"], writes=["G.e1"])
            ar = qi % 2
            dve(lambda e, o1=o1, ar=ar: e.tensor_scalar(
                G_AO(ar)[:, 0:128], o1[:, 0:128], EP1, None, op0=ALU.mult),
                reads=["ps4", "ps5", "G.e1"], writes=["G.ao%d" % ar])
            pe(lambda e, ar=ar: e.transpose(psbf(6, 128), G_AO(ar)[:, 0:128],
                                            IDENT_BF),
               reads=["G.ao%d" % ar, "identb"], writes=["ps6"])
            act(lambda e, ar=ar: e.copy(G_AO(ar)[:, 128:256], psbf(6, 128)),
                reads=["ps6"], writes=["G.aot%d" % ar])
            dma("act", afm[h * 128:(h + 1) * 128,
                           qoff + qi * 128: qoff + (qi + 1) * 128],
                G_AO(ar)[:, 128:256],
                reads=["G.aot%d" % ar], writes=["afm"])

    def g_emit_head(h, prod_steps):
        hr = h % 2
        g = h // 4
        scores, muls, pvs = [], [], []
        qb_end = {}
        it = 0
        for bi, (qoff, qn) in enumerate(BLKB):
            for tkc in range(13):
                nk = 128 if tkc < 12 else 4
                bank = 1 + it % 3
                slot = it % 3
                mr = it % 2

                def sc(g=g, hr=hr, tkc=tkc, qoff=qoff, qn=qn, bank=bank,
                       bi=bi, nk=nk, mr=mr):
                    dma("act", G_MSK(mr)[0:nk, :], maskb[bi, tkc, 0:nk, :],
                        writes=["G.mk%d" % mr])
                    mmg([(G_KB(g)[:, tkc * 128:tkc * 128 + nk],
                          G_QH(hr)[:, qoff:qoff + qn],
                          psv(bank, 512)[0:nk, :])],
                        reads=["G.kb%d" % g, "G.qh%d" % hr],
                        writes=["ps%d" % bank])

                def ex(h=h, bank=bank, slot=slot, nk=nk, mr=mr):
                    act(lambda e: e.activation(G_PT(slot)[0:nk, :],
                                               psv(bank, 512)[0:nk, :], AF.Exp,
                                               bias=BB(h)[0:nk, :], scale=SCALE_B),
                        reads=["ps%d" % bank, "Bb%d" % h], writes=["G.pt%d" % slot])
                    dve(lambda e: e.tensor_tensor(
                        G_PT(slot)[0:nk, :], G_PT(slot)[0:nk, :], G_MSK(mr)[0:nk, :],
                        op=ALU.mult),
                        reads=["G.mk%d" % mr, "G.pt%d" % slot],
                        writes=["G.pt%d" % slot])

                def pv(g=g, tkc=tkc, slot=slot, nk=nk):
                    pairs = []
                    for qi in range(4):
                        pairs.append((G_PT(slot)[0:nk, qi * 128:(qi + 1) * 128],
                                      G_VB[0:nk, tkc * 516 + g * 129:
                                           tkc * 516 + (g + 1) * 129],
                                      c_otile(0, qi),
                                      tkc == 0 and qi in (0, 3),
                                      tkc == 12 and qi in (2, 3)))
                    mmgf(pairs, reads=["G.pt%d" % slot, "G.vb"],
                         writes=["ps4", "ps5"])
                scores.append(sc); muls.append(ex); pvs.append(pv)
                it += 1
            qb_end[it - 1] = qoff
        N = len(scores)
        pi = [0]
        stride = max(2, (N + 4) // (len(prod_steps) + 1)) if prod_steps else N + 9

        def flush_pv(j):
            pvs[j]()
            if j in qb_end:
                g_epilogue(h, qb_end[j])
        for t in range(N):
            scores[t]()
            if t >= 1:
                muls[t - 1]()
            if t >= 2:
                flush_pv(t - 2)
            if prod_steps and t % stride == stride - 1 and pi[0] < len(prod_steps):
                prod_steps[pi[0]]()
                pi[0] += 1
        muls[N - 1]()
        flush_pv(N - 2)
        flush_pv(N - 1)
        while pi[0] < len(prod_steps):
            prod_steps[pi[0]]()
            pi[0] += 1

    for st in g_prod_steps(0):
        st()
    for h in range(H):
        g_emit_head(h, g_prod_steps(h + 1) if h + 1 < H else [])
    P.fence()

    for fc in range(FC):
        dma("sync", XH(fc, 0, TOWN), afm3[:, fc, 0:TOWN], reads=["afm"], writes=[XHN(fc)])
    dma("sync", sm[:, 432:448], bob, writes=["bobc"])
    for oc in range(16):
        orr = oc % 2
        dma("pool", D_WS(orr).rearrange("p (fc o) -> p fc o", fc=FC), wob[oc],
            writes=["Db.ws%d" % orr])
        for bi, (off, n) in enumerate(TCS_OWN):
            pk = bi % 2
            mmg([(D_WS(orr)[:, fc * 128:(fc + 1) * 128], XH(fc, off, n),
                  psv(2 * pk, n)) for fc in range(FC)],
                reads=["Db.ws%d" % orr] + [XHN(fc) for fc in range(FC)],
                writes=pnames(2 * pk, n))
            dve(lambda e, oc=oc, off=off, n=n, pk=pk: e.scalar_tensor_tensor(
                XC(oc, off, n), psv(2 * pk, n), sm[:, 432 + oc:433 + oc], XC(oc, off, n),
                op0=ALU.add, op1=ALU.add),
                reads=pnames(2 * pk, n) + [XN(oc), "bobc"], writes=[XN(oc)])
    if "x:attn_b" in taps:
        for fc in range(FC):
            dma("sync", tap_outs["x:attn_b"].rearrange("(fc p) t -> p fc t", p=128)[:, fc, :],
                XC(fc), reads=[XN(fc)], writes=["tapxb"])
    P.fence()

    ln_main("F2.", TCS_OWN, True)
    gate_and_moe("F2.", TOWN, TCS_OWN, T128_OWN)
    P.fence()

    for fc in range(FC):
        dma("sync", y3[:, fc, :], XC(fc, 0, TOWN), reads=[XN(fc)], writes=["y"])
    P.add("sync", lambda e: e.nop(), reads=["y"])

    P.emit(block, engsem, dmasem)
    es.close()
    return nc



_PREPPED = {}


def _prep(inputs):
    key = id(inputs.get("L0_attn_a_x"))
    x = np.asarray(inputs["L0_attn_a_x"], np.float32)
    wqa = _rhsT3(np.asarray(inputs["L0_attn_a_Wq"], np.float32))
    wka = _rhsT3(np.asarray(inputs["L0_attn_a_Wk"], np.float32))
    wva = _rhsT3(np.asarray(inputs["L0_attn_a_Wv"], np.float32))
    woa = _oslab(np.asarray(inputs["L0_attn_a_Wo"], np.float32))
    wqb = _rhsT3_em(np.asarray(inputs["L0_attn_b_Wq"], np.float32))
    wkb = _rhsT3_em(np.asarray(inputs["L0_attn_b_Wk"], np.float32))
    wvb = _rhsT3_em(np.asarray(inputs["L0_attn_b_Wv"], np.float32))
    wob = _oslab_em(np.asarray(inputs["L0_attn_b_Wo"], np.float32))
    gate_sl = np.ascontiguousarray(
        np.asarray(inputs["L0_ffn_gate_w"], np.float32).T.reshape(FC, 128, NEXP))
    wg = _guslab(np.asarray(inputs["L0_ffn_w_gate"], np.float32))
    wu = _guslab(np.asarray(inputs["L0_ffn_w_up"], np.float32))
    wd = _dslab(np.asarray(inputs["L0_ffn_w_down"], np.float32))

    com = {
        "wqa": wqa, "wka": wka, "wva": wva, "woa": woa,
        "wqb": wqb, "wkb": wkb, "wvb": wvb, "wob": wob,
        "bqb": np.asarray(inputs["L0_attn_b_bq"], np.float32).reshape(1, E),
        "bkb": np.asarray(inputs["L0_attn_b_bk"], np.float32).reshape(1, 512),
        "bvb": np.asarray(inputs["L0_attn_b_bv"], np.float32).reshape(1, 512),
        "bob": np.ascontiguousarray(
            np.asarray(inputs["L0_attn_b_bo"], np.float32).reshape(16, 128).T),
        "gate_sl": gate_sl, "wg": wg, "wu": wu, "wd": wd,
        "lq1": np.asarray(inputs["L0_attn_a_lambda_q1"], np.float32).reshape(1, 64),
        "lk1": np.asarray(inputs["L0_attn_a_lambda_k1"], np.float32).reshape(1, 64),
        "lq2": np.asarray(inputs["L0_attn_a_lambda_q2"], np.float32).reshape(1, 64),
        "lk2": np.asarray(inputs["L0_attn_a_lambda_k2"], np.float32).reshape(1, 64),
        "gam": np.asarray(inputs["L0_attn_a_gamma"], np.float32).reshape(1, 128),
        "ident": np.eye(128, dtype=np.float32),
        "identb": np.eye(128, dtype=np.float32).astype(ml_dtypes.bfloat16),
    }
    BF = ml_dtypes.bfloat16
    seq_pos = np.arange(TSEQ)
    ca_seq, sa_seq = _rope_a_tables(seq_pos)
    com["ca_seq"], com["sa_seq"] = ca_seq.astype(BF), sa_seq.astype(BF)

    in_maps = []
    for core in range(8):
        s, j = core // 4, core % 4
        xT = np.ascontiguousarray(x[s].T)
        own = xT[:, j * TOWN:(j + 1) * TOWN]
        ext = (xT[:, j * TOWN - TEXT: j * TOWN] if j > 0
               else np.zeros((E, TEXT), np.float32))
        sink = xT[:, 0:TSINK]
        x_proc = np.ascontiguousarray(np.concatenate([own, ext, sink], axis=1))
        pos = np.concatenate([
            j * TOWN + np.arange(TOWN),
            (j * TOWN - TEXT + np.arange(TEXT)) if j > 0 else np.zeros(TEXT, np.int64),
            np.arange(TSINK)])
        ca_pr, sa_pr = _rope_a_tables(pos)
        cb_pr, sb_pr = _rope_b_tables(pos)
        m = dict(com)
        m["x_fm"] = np.ascontiguousarray(xT)
        m["x_proc"] = x_proc
        m["ca_pr"], m["sa_pr"] = ca_pr.astype(BF), sa_pr.astype(BF)
        m["cb_pr"], m["sb_pr"] = cb_pr.astype(BF), sb_pr.astype(BF)
        m["maskb"] = _attnb_masks(j).astype(BF)
        in_maps.append(m)
    return in_maps


def kernel(**inputs):
    in_maps = _prep(inputs)
    nc = build()
    res = run_bass_kernel_spmd(nc, in_maps, core_ids=list(range(8)))
    out = np.empty((2, TSEQ, E), np.float32)
    for core in range(8):
        s, j = core // 4, core % 4
        out[s, j * TOWN:(j + 1) * TOWN, :] = res.results[core]["y"].T
    return out
